# version 14
# baseline (speedup 1.0000x reference)
import contextlib
import numpy as np
import ml_dtypes
import concourse.bass as bass
import concourse.mybir as mybir
from concourse.bass_utils import run_bass_kernel_spmd

F32 = mybir.dt.float32
BF16 = mybir.dt.bfloat16
I32 = mybir.dt.int32
AF = mybir.ActivationFunctionType
OP = mybir.AluOpType
AX = mybir.AxisListType
BF = ml_dtypes.bfloat16

B, T, D, H = 8, 2048, 1024, 16
NCH = D // 128
NTT = T // 512
FF = 4096
NFC = FF // 128
EPS = 1e-5


def build_nc(dbg=False):
    nc = bass.Bass("TRN2", target_bir_lowering=False)
    d = {}

    def din(name, shape, dt):
        d[name] = nc.dram_tensor(name, list(shape), dt, kind="ExternalInput")

    din("xT", (D, T), F32)
    din("wqkv", (D, 3 * D), F32)
    din("bqkv", (3 * D,), F32)
    din("wproj", (D, D), F32)
    din("bproj", (D,), F32)
    din("wfcT", (D, FF), F32)
    din("bfc", (FF,), F32)
    din("afcT", (D, 16), F32)
    din("BfcT", (16, FF), F32)
    din("wpjT", (FF, D), F32)
    din("bpj2", (D,), F32)
    din("apjT", (FF, 16), F32)
    din("BpjT", (16, D), F32)
    din("tri", (128, 128), BF16)
    din("iden", (128, 128), F32)
    din("onespart", (128, 128), F32)
    din("onescolb", (1, 128), BF16)
    din("onescol128b", (128, 1), BF16)
    din("oneblk", (128, 33), BF16)
    din("o1024col", (128, 1), F32)
    din("o1024colb", (128, 1), BF16)
    din("onesrow512b", (1, 512), BF16)
    yT = nc.dram_tensor("yT", [D, T], F32, kind="ExternalOutput")
    dbgt = {}
    if dbg:
        for nm, shape, dt in [("dbg_xln", (128, NCH, T), BF16),
                              ("dbg_q", (128, T), BF16), ("dbg_k", (128, T), BF16),
                              ("dbg_vt", (128, 16, 128), BF16),
                              ("dbg_bias", (128, 2), F32),
                              ("dbg_pb", (128, 1024), BF16),
                              ("dbg_aout", (128, NCH, T), BF16),
                              ("dbg_x1", (128, NCH, T), F32),
                              ("dbg_qsc", (128, 8), F32),
                              ("dbg_qrow", (1, 8), F32),
                              ("dbg_tfc", (16, T), BF16),
                              ("dbg_ht", (128, NFC, 1024), BF16),
                              ("dbg_rowm", (1, T), F32),
                              ("dbg_rowq2", (1, T), F32),
                              ("dbg_rowrstd", (1, T), F32),
                              ("dbg_mh", (128, T), F32),
                              ("dbg_sh", (128, T), F32),
                              ("dbg_wtl", (128, NCH, 3, 128), BF16),
                              ("dbg_brow", (1, 3 * D), BF16)]:
            dbgt[nm] = nc.dram_tensor(nm, list(shape), dt, kind="ExternalOutput")
    wfcq = nc.dram_tensor("wfcq_scratch", [D, FF], BF16)
    wpqq = nc.dram_tensor("wpq_scratch", [FF, D], BF16)

    st = contextlib.ExitStack()
    sb = lambda n, s, t, stk=None: (stk or st).enter_context(
        nc.sbuf_tensor(n, list(s), t))
    A = bass.AP

    xs = sb("xs", (128, NCH, T), F32)
    xln = sb("xln", (128, NCH, T), BF16)
    sqtmp = sb("sqtmp", (128, 1024), BF16)
    sqtmp2 = sb("sqtmp2", (128, 1024), BF16)
    tri = sb("z_tri", (128, 128), BF16)
    iden = sb("z_iden", (128, 128), F32)
    onespart = sb("z_onespart", (128, 128), F32)
    onescolb = sb("z_onescolb", (1, 128), BF16)
    onescol128b = sb("z_onescol128b", (128, 1), BF16)
    oneblk = sb("z_oneblk", (128, 33), BF16)
    o1024col = sb("z_o1024col", (128, 1), F32)
    o1024colb = sb("z_o1024colb", (128, 1), BF16)
    onesrow512b = sb("z_onesrow512b", (1, 512), BF16)
    bqkv_c = sb("bqkv_c", (128, 24), F32)
    bproj_c = sb("bproj_c", (128, 8), F32)
    bfc_c = sb("bfc_c", (128, 32), F32)
    bpj2_rowb = sb("bpj2_rowb", (1, D), BF16)
    afc_s = sb("afc_s", (128, NCH, 16), BF16)
    apj_s = sb("apj_s", (128, NFC, 16), BF16)
    nrm = sb("nrm", (33, 8), F32)
    nrmt = sb("nrmt", (33, 12), F32)
    biasEO = sb("biasEO", (128, 2), F32)
    qsc = sb("qsc", (128, 8), F32)
    qrow = sb("qrow", (1, 8), F32)
    barscr = sb("barscr", (1, 8), F32)
    fillscr = sb("fillscr", (1, 16), F32)

    ps1 = st.enter_context(nc.psum_tensor("ps1", [128, 512], F32))
    ps2 = st.enter_context(nc.psum_tensor("ps2", [128, 512], F32))
    ps3 = st.enter_context(nc.psum_tensor("ps3", [128, 1024], F32))
    ps4 = st.enter_context(nc.psum_tensor("ps4", [128, 1024], F32))
    ps5 = st.enter_context(nc.psum_tensor("ps5", [128, 512], F32))
    ps6 = st.enter_context(nc.psum_tensor("ps6", [128, 512], F32))

    sems = {}
    cnt = {}
    for s in ("DI", "DG", "MM", "VV", "SS", "BR", "GG", "DA", "DB", "DC", "DD",
              "X0", "X1", "X2", "X3", "X4", "X5", "X6", "X7",
              "W0", "W1", "W2", "W3"):
        sems[s] = st.enter_context(nc.semaphore(s))
        cnt[s] = 0
    ENGS = ("tensor", "vector", "scalar", "gpsimd", "sync")
    prog = {k: [] for k in ENGS}

    def op(eng, fn, inc=None, n=1):
        tgt = None
        if inc is not None:
            cnt[inc] += n
            tgt = cnt[inc]

            def thunk(e, fn=fn, s=inc, n=n):
                fn(e).then_inc(sems[s], n)
        else:
            def thunk(e, fn=fn):
                fn(e)
        prog[eng].append(thunk)
        return tgt

    def wt(eng, sem, tgt):
        prog[eng].append(lambda e, s=sem, t=tgt: e.wait_ge(sems[s], t))

    def bump(eng, sem):
        cnt[sem] += 1
        tgt = cnt[sem]
        prog[eng].append(lambda e, s=sem: e.nop().then_inc(sems[s], 1))
        return tgt

    def dma(eng, dst, src, sem, slow=False):
        cnt[sem] += 16
        tgt = cnt[sem]
        prog[eng].append(
            lambda e, d2=dst, s2=src, s=sem, sl=slow: e.dma_start(
                d2, s2, allow_slow_non_contiguous=sl).then_inc(sems[s], 16))
        return tgt

    nbar = [0]
    nfil = [0]

    def vfill(n=4):
        for _ in range(n):
            nfil[0] += 1
            sl = nfil[0] % 8
            prog["vector"].append(
                lambda e, sl=sl: e.tensor_copy(fillscr[0:1, 8 + sl:9 + sl],
                                               fillscr[0:1, sl:sl + 1]))

    def barrier():
        nbar[0] += 1
        tgt = 5 * nbar[0]
        cnt["BR"] = tgt
        for eng in ENGS:
            for s_ in ("DI", "DG", "GG", "DA", "DB", "DC", "DD",
                       "X0", "X1", "X2", "X3", "X4", "X5", "X6", "X7",
                       "W0", "W1", "W2", "W3"):
                wt(eng, s_, cnt[s_])
        wt("tensor", "MM", cnt["MM"])
        wt("tensor", "SS", cnt["SS"])
        prog["tensor"].append(lambda e: e.nop().then_inc(sems["BR"], 1))
        prog["sync"].append(lambda e: e.nop().then_inc(sems["BR"], 1))
        prog["vector"].append(
            lambda e: e.memset(barscr[0:1, 0:1], 0.0).then_inc(sems["BR"], 1))
        prog["scalar"].append(
            lambda e: e.memzero(barscr[0:1, 2:3]).then_inc(sems["BR"], 1))
        prog["gpsimd"].append(
            lambda e: e.tensor_copy(barscr[0:1, 4:5], barscr[0:1, 5:6]
                                    ).then_inc(sems["BR"], 1))
        for eng in ENGS:
            wt(eng, "BR", tgt)

    prog["vector"].append(lambda e: e.memset(fillscr[0:1, 0:8], 0.0))
    prog["gpsimd"].append(lambda e: e.memset(barscr[0:1, 5:6], 0.0))
    for name, sbt in [("tri", tri), ("iden", iden), ("onespart", onespart),
                      ("onescolb", onescolb), ("oneblk", oneblk),
                      ("o1024col", o1024col), ("o1024colb", o1024colb),
                      ("onesrow512b", onesrow512b), ("onescol128b", onescol128b)]:
        dma("sync", sbt[:, :], d[name][:, :], "DI")
    xdma = {}
    for tt in range(NTT):
        for par in range(2):
            s_ = f"X{tt * 2 + par}"
            xdma[(tt, par)] = (s_, dma(
                ["sync", "scalar"][par],
                A(xs, par * T + tt * 512, [[NCH * T, 128], [2 * T, 4], [1, 512]]),
                A(d["xT"], par * 128 * T + tt * 512,
                  [[T, 128], [2 * 128 * T, 4], [1, 512]]),
                s_))
    dma("sync", bqkv_c[:, :], A(d["bqkv"], 0, [[1, 128], [128, 24]]), "GG", slow=True)
    dma("sync", bproj_c[:, :], A(d["bproj"], 0, [[1, 128], [128, 8]]), "GG", slow=True)
    dma("sync", bfc_c[:, :], A(d["bfc"], 0, [[1, 128], [128, 32]]), "GG", slow=True)
    dma("gpsimd", afc_s[:, :, :],
        A(d["afcT"], 0, [[16, 128], [128 * 16, NCH], [1, 16]]), "DG")
    t_apj = dma("gpsimd", apj_s[:, :, :],
                A(d["apjT"], 0, [[16, 128], [128 * 16, NFC], [1, 16]]), "DG")

    ln_idx = [0]

    def layernorm():
        ln_idx[0] += 1
        sfx = str(ln_idx[0])
        stL = contextlib.ExitStack()
        Mh = sb("Mh" + sfx, (128, T), F32, stL)
        Sh = sb("Sh" + sfx, (128, T), F32, stL)
        rowm = sb("rowm" + sfx, (1, T), F32, stL)
        rowq2 = sb("rowq2" + sfx, (1, T), F32, stL)
        rowvar = sb("rowvar" + sfx, (1, T), F32, stL)
        rowln = sb("rowln" + sfx, (1, T), F32, stL)
        rowrstd = sb("rowrstd" + sfx, (1, T), F32, stL)
        lnscr = sb("lnscrL" + sfx, (128, 2048), F32, stL)
        wt("tensor", "DI", cnt["DI"])
        wt("tensor", "VV", cnt["VV"])
        wt("scalar", "DI", cnt["DI"])
        wt("scalar", "VV", cnt["VV"])
        wt("vector", "SS", cnt["SS"])
        slotmm = {0: 0, 1: 0}
        rowevac_mark = {0: 0, 1: 0}
        expevac_mark = {0: 0, 1: 0}
        first_ln = (ln_idx[0] == 1)
        pend_expand = [None]
        for tt in range(NTT):
            psmA = [ps1, ps2][tt % 2]
            psmB = [ps5, ps6][tt % 2]
            if tt >= 2:
                wt("tensor", "VV", rowevac_mark[tt % 2])
            for c in range(NCH):
                if c == 3 and pend_expand[0]:
                    pend_expand[0]()
                    pend_expand[0] = None
                sl = (c % 2) * 512
                eng = ["scalar", "vector"][c % 2]
                sem = ["SS", "VV"][c % 2]
                if slotmm[c % 2]:
                    wt(eng, "MM", slotmm[c % 2])
                if first_ln:
                    sx, vx = xdma[(tt, c % 2)]
                    wt(eng, sx, vx)
                if eng == "scalar":
                    op("scalar", lambda e, c=c, tt=tt, sl=sl: e.activation(
                        sqtmp[:, sl:sl + 512], xs[:, c, tt * 512:(tt + 1) * 512],
                        AF.Copy))
                    op("scalar", lambda e, c=c, tt=tt, sl=sl: e.activation(
                        sqtmp2[:, sl:sl + 512], xs[:, c, tt * 512:(tt + 1) * 512],
                        AF.Square), inc="SS")
                else:
                    op("vector", lambda e, c=c, tt=tt, sl=sl: e.tensor_copy(
                        sqtmp[:, sl:sl + 512], xs[:, c, tt * 512:(tt + 1) * 512]))
                    op("vector", lambda e, c=c, tt=tt, sl=sl: e.tensor_tensor(
                        sqtmp2[:, sl:sl + 512], xs[:, c, tt * 512:(tt + 1) * 512],
                        xs[:, c, tt * 512:(tt + 1) * 512], OP.mult), inc="VV")
                wt("tensor", sem, cnt[sem])
                op("tensor", lambda e, c=c, p=psmA, sl=sl: e.matmul(
                    p[0:1, 0:512], o1024colb[:, 0:1], sqtmp[:, sl:sl + 512],
                    start=(c == 0), stop=(c == NCH - 1)))
                op("tensor", lambda e, c=c, p=psmB, sl=sl: e.matmul(
                    p[0:1, 0:512], o1024colb[:, 0:1], sqtmp2[:, sl:sl + 512],
                    start=(c == 0), stop=(c == NCH - 1)), inc="MM")
                slotmm[c % 2] = cnt["MM"]
            ts = tt * 512
            wt("vector", "MM", cnt["MM"])
            op("vector", lambda e, p=psmA, ts=ts: e.tensor_copy(
                rowm[0:1, ts:ts + 512], p[0:1, 0:512]))
            op("vector", lambda e, p=psmB, ts=ts: e.tensor_copy(
                rowq2[0:1, ts:ts + 512], p[0:1, 0:512]), inc="VV")
            rowevac_mark[tt % 2] = cnt["VV"]
            wt("vector", "VV", cnt["VV"])
            op("vector", lambda e, ts=ts: e.tensor_mul(
                rowvar[0:1, ts:ts + 512], rowm[0:1, ts:ts + 512],
                rowm[0:1, ts:ts + 512]), inc="VV")
            wt("vector", "VV", cnt["VV"])
            op("vector", lambda e, ts=ts: e.scalar_tensor_tensor(
                rowvar[0:1, ts:ts + 512], rowq2[0:1, ts:ts + 512], EPS,
                rowvar[0:1, ts:ts + 512], OP.add, OP.subtract), inc="VV")
            wt("scalar", "VV", cnt["VV"])
            op("scalar", lambda e, ts=ts: e.activation(
                rowln[0:1, ts:ts + 512], rowvar[0:1, ts:ts + 512], AF.Ln),
                inc="SS")
            wt("scalar", "SS", cnt["SS"])
            op("scalar", lambda e, ts=ts: e.activation(
                rowrstd[0:1, ts:ts + 512], rowln[0:1, ts:ts + 512], AF.Exp,
                scale=-0.5), inc="SS")
            def emit_expand(tt=tt, ts=ts, sm=cnt["SS"]):
                eb = [ps3, ps4][tt % 2]
                wt("tensor", "SS", sm)
                if expevac_mark[tt % 2]:
                    wt("tensor", "VV", expevac_mark[tt % 2])
                op("tensor", lambda e, eb=eb, ts=ts: e.matmul(
                    eb[:, 0:512], onespart[0:1, :], rowm[0:1, ts:ts + 512],
                    start=True, stop=True))
                op("tensor", lambda e, eb=eb, ts=ts: e.matmul(
                    eb[:, 512:1024], onespart[0:1, :],
                    rowrstd[0:1, ts:ts + 512],
                    start=True, stop=True), inc="MM")
                wt("vector", "MM", cnt["MM"])
                op("vector", lambda e, eb=eb, ts=ts: e.tensor_copy(
                    Mh[:, ts:ts + 512], eb[:, 0:512]))
                op("vector", lambda e, eb=eb, ts=ts: e.tensor_copy(
                    Sh[:, ts:ts + 512], eb[:, 512:1024]), inc="VV")
                expevac_mark[tt % 2] = cnt["VV"]
            if tt < NTT - 1:
                pend_expand[0] = emit_expand
            else:
                emit_expand()
        if first_ln:
            for k_ in range(8):
                wt("vector", f"X{k_}", cnt[f"X{k_}"])
        if dbg and ln_idx[0] == 1:
            wt("sync", "VV", cnt["VV"])
            dma("sync", dbgt["dbg_rowm"][:, :], rowm[:, :], "DI")
            dma("sync", dbgt["dbg_rowq2"][:, :], rowq2[:, :], "DI")
            dma("sync", dbgt["dbg_rowrstd"][:, :], rowrstd[:, :], "DI")
            dma("sync", dbgt["dbg_mh"][:, :], Mh[:, :], "DI")
            dma("sync", dbgt["dbg_sh"][:, :], Sh[:, :], "DI")
            wt("vector", "DI", cnt["DI"])
        pairs = [(c, tt) for c in range(NCH) for tt in range(NTT)]
        for g in range(0, len(pairs), 4):
            grp = pairs[g:g + 4]
            wt("vector", "VV", cnt["VV"])
            for i, (c, tt) in enumerate(grp):
                op("vector", lambda e, c=c, tt=tt, i=i: e.tensor_sub(
                    lnscr[:, i * 512:(i + 1) * 512],
                    xs[:, c, tt * 512:(tt + 1) * 512],
                    Mh[:, tt * 512:(tt + 1) * 512]),
                    inc="VV" if (c, tt) == grp[-1] else None)
            wt("vector", "VV", cnt["VV"])
            for i, (c, tt) in enumerate(grp):
                op("vector", lambda e, c=c, tt=tt, i=i: e.tensor_mul(
                    xln[:, c, tt * 512:(tt + 1) * 512],
                    lnscr[:, i * 512:(i + 1) * 512],
                    Sh[:, tt * 512:(tt + 1) * 512]),
                    inc="VV" if (c, tt) == grp[-1] else None)
        v = cnt["VV"]
        barrier()
        stL.close()
        return v

    v_ln1 = layernorm()
    if dbg:
        wt("sync", "VV", v_ln1)
        dma("sync", dbgt["dbg_xln"][:, :, :], xln[:, :, :], "DI")
        wt("tensor", "DI", cnt["DI"])

    stA = contextlib.ExitStack()
    stA1 = contextlib.ExitStack()
    aout = sb("aout", (128, NCH, T), BF16, stA)
    stg = sb("stg", (128, 1024), F32, stA)
    stg2 = sb("stg2", (128, 1024), F32, stA)
    bqkv_rowb = sb("bqkv_rowb", (1, 3 * D), BF16, stA)
    stgB = sb("stgB", (128, 1024), F32, stA)
    stgB2 = sb("stgB2", (128, 1024), F32, stA)
    qi = sb("qi", (128, 1024), I32, stA)
    qb = sb("qb", (128, 1024), BF16, stA)
    wmax32 = sb("wmax32", (128, 64), F32, stA)
    srow128 = sb("srow128", (1, 128), F32, stA)
    Qp = sb("Qp", (128, T), BF16, stA1)
    Kp = sb("Kp", (128, T), BF16, stA1)
    Vt = sb("Vt", (128, 16, 132), BF16, stA1)
    op("vector", lambda e: e.memset(Vt[:, :, 64:66], 1.0))
    op("vector", lambda e: e.memset(Vt[:, :, 130:132], 1.0), inc="VV")
    wtl = [sb(f"wtl{i}", (128, NCH, 3, 128), BF16, stA1) for i in range(2)]
    PB = [sb(f"PB{i}", (128, 1024), BF16, stA1) for i in range(2)]
    recrow = sb("recrow", (128, 512), F32, stA1)
    recexp = sb("recexp", (128, 512), F32, stA1)
    t_brow = dma("gpsimd", bqkv_rowb[:, :], A(d["bqkv"], 0, [[3 * D, 1], [1, 3 * D]]),
                 "DG")

    def dma_wtl(p):
        buf = wtl[p % 2]
        t = None
        for j, coff in enumerate([128 * p, D + 128 * p, 2 * D + 128 * p]):
            t = dma("gpsimd", buf[:, :, j, :],
                    A(d["wqkv"], coff, [[3 * D, 128], [128 * 3 * D, NCH], [1, 128]]),
                    "DG")
        return t

    qsc_mark = {0: 0, 1: 0}
    QCH = 1024
    qfill_sem = ["DA", "DB"]
    qfillB_sem = ["DC", "DD"]

    def quant_passA_units(w, src_name, nrows, inner):
        nfill = nrows * inner // (128 * QCH)
        per_stage = inner // QCH
        def ap_of(f):
            g = f // per_stage
            cc = f % per_stage
            return A(d[src_name], g * 128 * inner + cc * QCH,
                     [[inner, 128], [1, QCH]])
        fills = {}
        marks = {}

        def fill(f):
            if f < 2:
                wt("sync", "VV", cnt["VV"])
            else:
                wt("sync", "VV", marks[f - 2])
            fills[f] = dma("sync", [stg, stg2][f % 2][:, :], ap_of(f),
                           qfill_sem[f % 2])
        def prime():
            fill(0)
            fill(1)
        units = []
        for f in range(nfill):
            def unit(f=f):
                s_ = [stg, stg2][f % 2]
                wt("vector", qfill_sem[f % 2], fills[f])
                if f < 2:
                    wt("vector", "VV", cnt["VV"])
                op("vector", lambda e, s_=s_, f=f: e.tensor_reduce(
                    wmax32[:, f:f + 1], s_[:, :], AX.X, OP.max,
                    apply_absolute_value=True), inc="VV")
                marks[f] = cnt["VV"]
                if f + 2 < nfill:
                    fill(f + 2)
            units.append(unit)
        return prime, units

    def quant_finalize(w, nfill):
        wt("vector", "VV", cnt["VV"])
        width = nfill
        srcbase = 0
        dstbase = 32
        while width > 1:
            half = (width + 1) // 2
            for j in range(half):
                a_ = srcbase + 2 * j
                b_ = min(srcbase + 2 * j + 1, srcbase + width - 1)
                op("vector", lambda e, a_=a_, b_=b_, j=j, dstbase=dstbase:
                    e.tensor_tensor(wmax32[:, dstbase + j:dstbase + j + 1],
                                    wmax32[:, a_:a_ + 1], wmax32[:, b_:b_ + 1],
                                    OP.max),
                    inc="VV" if j == half - 1 else None)
            wt("vector", "VV", cnt["VV"])
            srcbase, dstbase = dstbase, (0 if dstbase == 32 else 32)
            width = half
        fin = srcbase
        wt("tensor", "VV", cnt["VV"])
        op("tensor", lambda e, fin=fin: e.transpose(
            ps3[0:1, 0:128], wmax32[:, fin:fin + 1], iden[:, :]), inc="MM")
        wt("vector", "MM", cnt["MM"])
        op("vector", lambda e: e.tensor_copy(srow128[0:1, :], ps3[0:1, 0:128]),
           inc="VV")
        wt("vector", "VV", cnt["VV"])
        op("vector", lambda e, w=w: e.tensor_reduce(
            qrow[0:1, 2 * w:2 * w + 1], srow128[0:1, :], AX.X, OP.max), inc="VV")
        wt("vector", "VV", cnt["VV"])
        op("vector", lambda e, w=w: e.reciprocal(qrow[0:1, 4:5],
                                                 qrow[0:1, 2 * w:2 * w + 1]))
        op("vector", lambda e, w=w: e.tensor_scalar(
            qrow[0:1, 2 * w + 1:2 * w + 2], qrow[0:1, 2 * w:2 * w + 1],
            1.0 / 127.0, None, OP.mult), inc="VV")
        wt("vector", "VV", cnt["VV"])
        op("vector", lambda e, w=w: e.tensor_scalar(
            qrow[0:1, 2 * w:2 * w + 1], qrow[0:1, 4:5], 127.0, None, OP.mult),
            inc="VV")
        wt("tensor", "VV", cnt["VV"])
        op("tensor", lambda e, w=w: e.matmul(
            ps3[:, 0:1], onespart[0:1, :], qrow[0:1, 2 * w:2 * w + 1],
            start=True, stop=True))
        op("tensor", lambda e, w=w: e.matmul(
            ps3[:, 1:2], onespart[0:1, :], qrow[0:1, 2 * w + 1:2 * w + 2],
            start=True, stop=True), inc="MM")
        wt("vector", "MM", cnt["MM"])
        op("vector", lambda e, w=w: e.tensor_copy(qsc[:, 2 * w:2 * w + 2],
                                                  ps3[:, 0:2]), inc="VV")
        qsc_mark[w] = cnt["VV"]
        wt("tensor", "VV", cnt["VV"])

    def quant_passB_units(w, src_name, nrows, inner, scr):
        nfill = nrows * inner // (128 * QCH)
        per_stage = inner // QCH
        def ap_of(f, dr=False):
            g = f // per_stage
            cc = f % per_stage
            t = scr if dr else d[src_name]
            return A(t, g * 128 * inner + cc * QCH, [[inner, 128], [1, QCH]])
        fills = {}
        marks = {}

        def fill(f):
            if f < 2:
                wt("sync", "SS", cnt["SS"])
                wt("sync", "VV", cnt["VV"])
            else:
                wt("sync", "SS", marks[f - 2])
            fills[f] = dma("sync", [stgB, stgB2][f % 2][:, :], ap_of(f),
                           qfillB_sem[f % 2])
        def prime():
            fill(0)
            fill(1)
        units = []
        for f in range(nfill):
            def unit(f=f):
                s_ = [stgB, stgB2][f % 2]
                wt("scalar", qfillB_sem[f % 2], fills[f])
                wt("scalar", "DI", cnt["DI"])
                wt("scalar", "SS", cnt["SS"])
                wt("scalar", "VV", qsc_mark[w])
                op("scalar", lambda e, s_=s_, w=w: e.activation(
                    qi[:, 0:1024], s_[:, 0:1024], AF.Copy,
                    scale=qsc[:, 2 * w:2 * w + 1]), inc="SS")
                wt("scalar", "SS", cnt["SS"])
                op("scalar", lambda e: e.activation(
                    qb[:, 0:1024], qi[:, 0:1024], AF.Copy), inc="SS")
                marks[f] = cnt["SS"]
                wt("sync", "SS", cnt["SS"])
                dma("sync", ap_of(f, dr=True), qb[:, :], "DI")
                if f + 2 < nfill:
                    fill(f + 2)
            units.append(unit)
        return prime, units

    _pj_A_prime, _pj_A = quant_passA_units(1, "wpjT", NFC * 128, D)
    _fc_A_prime, _fc_A = quant_passA_units(0, "wfcT", NCH * 128, FF)
    _fc_B_prime, _fc_B = quant_passB_units(0, "wfcT", NCH * 128, FF, wfcq)
    _pj_B_prime, _pj_B = quant_passB_units(1, "wpjT", NFC * 128, D, wpqq)
    qworkA = (_fc_A + ["FIN_FC"] + _pj_A + ["FIN_PJ"])
    qworkB = (_fc_B + _pj_B)
    qposA = [0]
    qposB = [0]

    def qpump(n, fin=False):
        while n > 0 and qposA[0] < len(qworkA):
            u = qworkA[qposA[0]]
            if u == "FIN_PJ" or u == "FIN_FC":
                if not fin:
                    return
                qposA[0] += 1
                quant_finalize(1 if u == "FIN_PJ" else 0, 32)
                if u == "FIN_FC":
                    _pj_A_prime()
            else:
                qposA[0] += 1
                u()
                n -= 1

    def qpumpB(n):
        while n > 0 and qposB[0] < len(qworkB):
            w_ = 0 if qposB[0] < len(_fc_B) else 1
            if qsc_mark[w_] == 0:
                return
            if qposB[0] == len(_fc_B):
                _pj_B_prime()
            u = qworkB[qposB[0]]
            qposB[0] += 1
            u()
            n -= 1

    wtl_t = {0: dma_wtl(0)}
    _fc_A_prime()
    _fc_B_prime()
    av_mm_for_buf = {0: 0, 1: 0}
    mul_mark = {0: 0, 1: 0}
    qk_mark = {0: 0, 1: 0}
    v_mark = {0: 0, 1: 0}
    exp_hist = [0]
    csmm_mark = {}
    red_mark = {0: 0, 1: 0}
    norm_pend = [None]
    biasEO_mark = [0]
    rrfree = {0: 0, 1: 0}

    if dbg:
        wt("sync", "DG", cnt["DG"])
        dma("sync", dbgt["dbg_wtl"][:, :, :, :], wtl[0][:, :, :, :], "DI")
        wt("sync", "DG", cnt["DG"])
        dma("sync", dbgt["dbg_brow"][:, :], bqkv_rowb[:, :], "DI")
        wt("tensor", "DI", cnt["DI"])
    for p in range(H // 2):
        buf = p % 2
        wt("tensor", "DG", cnt["DG"])
        wt("tensor", "DG", cnt["DG"])
        wt("tensor", "VV", v_ln1)
        qpump(1, fin=True)
        def emit_cs(w, tt, buf=buf):
            csb = [ps3, ps4][tt % 2]
            sq_ = [sqtmp, sqtmp2][tt // 2]
            sl = (tt % 2) * 512
            wt("tensor", "SS", sq_mark[(w, tt // 2)])
            if red_mark[tt % 2]:
                wt("tensor", "VV", red_mark[tt % 2])
            if exp_hist[0]:
                wt("tensor", "SS", exp_hist[0])
            op("tensor", lambda e, csb=csb, sq_=sq_, sl=sl: e.matmul(
                csb[0:1, 0:512], onescol128b[:, 0:1], sq_[:, sl:sl + 512],
                start=True, stop=True), inc="MM")
            csmm_mark[(w, tt)] = cnt["MM"]
            wt("vector", "MM", cnt["MM"])
            op("vector", lambda e, w=w, tt=tt, csb=csb: e.tensor_reduce(
                nrmt[0:1, 4 * w + tt:4 * w + tt + 1], csb[0:1, 0:512],
                AX.X, OP.max), inc="VV")
            red_mark[tt % 2] = cnt["VV"]

        sq_mark = {}

        def emit_group(w, tt, buf=buf):
            dst = [Qp, Kp][w]
            boff = 128 * p + w * D
            psm = [ps1, ps2][tt % 2]
            wt("tensor", "VV", qk_mark[tt % 2])
            for c in range(NCH):
                op("tensor", lambda e, c=c, tt=tt, p2=psm, w=w, buf=buf:
                    e.matmul(
                        p2[:, 0:512], wtl[buf][:, c, w, :],
                        xln[:, c, tt * 512:(tt + 1) * 512],
                        start=(c == 0), stop=False))
            op("tensor", lambda e, p2=psm, bo=boff: e.matmul(
                p2[:, 0:512], bqkv_rowb[0:1, bo:bo + 128],
                onesrow512b[0:1, 0:512], start=False, stop=True), inc="MM")
            wt("vector", "MM", cnt["MM"])
            op("vector", lambda e, tt=tt, p2=psm, dst=dst: e.tensor_copy(
                dst[:, tt * 512:(tt + 1) * 512], p2[:, 0:512]), inc="VV")
            qk_mark[tt % 2] = cnt["VV"]
            if tt % 2 == 1:
                half = tt // 2
                sq_ = [sqtmp, sqtmp2][half]
                wt("scalar", "VV", qk_mark[1])
                wt("scalar", "MM", csmm_mark.get((1 - w, 2 * half + 1), 0))
                op("scalar", lambda e, dst=dst, half=half, sq_=sq_:
                    e.activation(sq_[:, 0:1024],
                                 dst[:, half * 1024:(half + 1) * 1024],
                                 AF.Square), inc="SS")
                sq_mark[(w, half)] = cnt["SS"]

        emit_group(0, 0)
        emit_group(0, 1)
        emit_group(0, 2)
        emit_cs(0, 0)
        emit_group(0, 3)
        emit_cs(0, 1)
        emit_group(1, 0)
        emit_cs(0, 2)
        emit_group(1, 1)
        emit_cs(0, 3)
        emit_group(1, 2)
        emit_cs(1, 0)
        emit_group(1, 3)
        emit_cs(1, 1)
        if p + 1 < H // 2:
            wt("gpsimd", "MM", cnt["MM"])
            wtl_t[p + 1] = dma_wtl(p + 1)
        if norm_pend[0]:
            norm_pend[0]()
            norm_pend[0] = None
        qpumpB(1)
        wt("vector", "VV", cnt["VV"])
        wt("tensor", "DG", cnt["DG"])
        wt("tensor", "VV", mul_mark[1])

        def emit_v(tc, buf=buf):
            psm = [ps5, ps6][tc % 2]
            wt("tensor", "VV", v_mark[tc % 2])
            for c in range(NCH):
                op("tensor", lambda e, c=c, tc=tc, p2=psm, buf=buf: e.matmul(
                    p2[:, 0:128], xln[:, c, tc * 128:(tc + 1) * 128],
                    wtl[buf][:, c, 2, :], start=(c == 0), stop=False))
            op("tensor", lambda e, tc=tc, p2=psm, p=p: e.matmul(
                p2[:, 0:128], onescolb[:, :],
                bqkv_rowb[0:1, 2 * D + 128 * p:2 * D + 128 * p + 128],
                start=False, stop=True), inc="MM")
            wt("vector", "MM", cnt["MM"])
            op("vector", lambda e, tc=tc, p2=psm, buf=buf: e.tensor_copy(
                A(Vt, tc * 132, [[16 * 132, 128], [66, 2], [1, 64]]),
                A(p2, 0, [[512, 128], [64, 2], [1, 64]])), inc="VV")
            v_mark[tc % 2] = cnt["VV"]

        emit_v(0)
        emit_cs(1, 2)
        emit_v(1)
        emit_cs(1, 3)
        emit_v(2)
        emit_v(3)
        wt("vector", "VV", cnt["VV"])
        for w in range(2):
            op("vector", lambda e, w=w: e.tensor_tensor(
                nrmt[0:1, 8 + 2 * w:9 + 2 * w], nrmt[0:1, 4 * w:4 * w + 1],
                nrmt[0:1, 4 * w + 1:4 * w + 2], OP.max))
            op("vector", lambda e, w=w: e.tensor_tensor(
                nrmt[0:1, 9 + 2 * w:10 + 2 * w], nrmt[0:1, 4 * w + 2:4 * w + 3],
                nrmt[0:1, 4 * w + 3:4 * w + 4], OP.max),
                inc="VV" if w == 1 else None)
        wt("vector", "VV", cnt["VV"])
        for w in range(2):
            op("vector", lambda e, w=w: e.tensor_tensor(
                nrm[0:1, w:w + 1], nrmt[0:1, 8 + 2 * w:9 + 2 * w],
                nrmt[0:1, 9 + 2 * w:10 + 2 * w], OP.max),
                inc="VV" if w == 1 else None)
        wt("scalar", "VV", cnt["VV"])
        op("scalar", lambda e: e.activation(nrm[0:1, 4:6], nrm[0:1, 0:2], AF.Sqrt),
           inc="SS")
        wt("vector", "SS", cnt["SS"])
        op("vector", lambda e: e.tensor_mul(nrm[0:1, 6:7], nrm[0:1, 4:5],
                                            nrm[0:1, 5:6]), inc="VV")
        wt("vector", "VV", cnt["VV"])
        op("vector", lambda e: e.tensor_scalar(nrm[0:1, 7:8], nrm[0:1, 6:7],
                                               -0.125, None, OP.mult), inc="VV")
        emit_v(4)
        wt("tensor", "VV", cnt["VV"])
        op("tensor", lambda e: e.matmul(ps1[:, 0:1], onespart[0:1, :],
                                        nrm[0:1, 7:8], start=True, stop=True),
           inc="MM")
        wt("vector", "MM", cnt["MM"])
        if exp_hist[0]:
            wt("vector", "SS", exp_hist[0])
        op("vector", lambda e: e.tensor_copy(biasEO[:, 0:1], ps1[:, 0:1]), inc="VV")
        biasEO_mark[0] = cnt["VV"]
        for tc in range(5, 16):
            emit_v(tc)
        prev_normB = [None]
        for qt in range(NTT):
            kmax = 4 * (qt + 1)
            q0 = qt * 512
            pE = [ps1, ps5][qt % 2]
            pO = [ps2, ps6][qt % 2]
            mm_of = {}
            pbr_of = {}
            exp_of = {}

            def emit_phase2(kc):
                psS = [ps3, ps4][kc % 2]
                if qt == 0 and kc < 2 and red_mark[kc % 2]:
                    wt("tensor", "VV", red_mark[kc % 2])
                if kc >= 2:
                    wt("tensor", "SS", exp_of[kc - 2])
                elif exp_hist[0]:
                    wt("tensor", "SS", exp_hist[0])
                op("tensor", lambda e, kc=kc, buf=buf, q0=q0, pS=psS: e.matmul(
                    pS[:, 0:512], Kp[0:64, kc * 128:(kc + 1) * 128],
                    Qp[0:64, q0:q0 + 512], start=True, stop=True))
                op("tensor", lambda e, kc=kc, buf=buf, q0=q0, pS=psS: e.matmul(
                    pS[:, 512:1024], Kp[64:128, kc * 128:(kc + 1) * 128],
                    Qp[64:128, q0:q0 + 512], start=True, stop=True), inc="MM")
                mm_of[kc] = cnt["MM"]

            def emit_exp(kc):
                psS = [ps3, ps4][kc % 2]
                pbuf = kc % 2
                if qt == 0 and kc == 0:
                    wt("scalar", "VV", biasEO_mark[0])
                wt("scalar", "MM", max(mm_of[kc], av_mm_for_buf[pbuf]))
                op("scalar", lambda e, pb=PB[pbuf], pS=psS: e.activation(
                    pb[:, 0:1024], pS[:, 0:1024], AF.Exp, bias=biasEO[:, 0:1],
                    scale=0.125), inc="SS")
                exp_of[kc] = cnt["SS"]
                exp_hist[0] = cnt["SS"]
                if kc >= 4 * qt:
                    off = (kc - 4 * qt) * 128
                    wt("vector", "SS", exp_of[kc])
                    jobs = []
                    for ho in (0, 512):
                        if off > 0:
                            jobs.append(("z", ho))
                        jobs.append(("m", ho))
                    for i, (kind, ho) in enumerate(jobs):
                        lastj = (i == len(jobs) - 1)
                        if kind == "z":
                            op("vector", lambda e, pb=PB[pbuf], off=off, ho=ho:
                                e.memset(pb[:, ho:ho + off], 0.0),
                                inc="VV" if lastj else None)
                        else:
                            op("vector", lambda e, pb=PB[pbuf], off=off, ho=ho:
                                e.tensor_tensor(
                                    pb[:, ho + off:ho + off + 128],
                                    pb[:, ho + off:ho + off + 128],
                                    tri[:, :], OP.mult),
                                inc="VV" if lastj else None)
                    pbr_of[kc] = ("VV", cnt["VV"])
                else:
                    pbr_of[kc] = ("SS", exp_of[kc])

            def emit_av(kc):
                pbuf = kc % 2
                first = (kc == 0)
                last = (kc == kmax - 1)
                if first:
                    wt("tensor", "VV", mul_mark[qt % 2])
                    if qt == 0:
                        wt("tensor", "VV", biasEO_mark[0])
                    elif qt == 1:
                        wt("tensor", "VV", max(v_mark[0], v_mark[1]))
                wt("tensor", pbr_of[kc][0], pbr_of[kc][1])
                op("tensor", lambda e, pb=PB[pbuf], kc=kc, buf=buf, f=first,
                   l=last, pE=pE: e.matmul(
                    pE[0:65, 0:512], Vt[:, kc, 0:65],
                    pb[:, 0:512], start=f, stop=l))
                op("tensor", lambda e, pb=PB[pbuf], kc=kc, buf=buf, f=first,
                   l=last, pO=pO: e.matmul(
                    pO[64:128, 0:512], Vt[:, kc, 66:130],
                    pb[:, 512:1024], start=f, stop=l))
                op("tensor", lambda e, pb=PB[pbuf], f=first, l=last, pO=pO:
                    e.matmul(
                    pO[0:1, 0:512], onescol128b[:, 0:1],
                    pb[:, 512:1024], start=f, stop=l), inc="MM")
                av_mm_for_buf[pbuf] = cnt["MM"]

            emit_phase2(0)
            emit_exp(0)
            if prev_normB[0]:
                prev_normB[0]()
                prev_normB[0] = None
            if kmax > 1:
                emit_phase2(1)
                emit_exp(1)
            for kc in range(kmax):
                if kc + 2 < kmax:
                    emit_phase2(kc + 2)
                emit_av(kc)
                if kc % 2 == 1 and kc < 4 * qt - 2:
                    qpump(1)
                if kc + 2 < kmax:
                    emit_exp(kc + 2)
            rr = recrow
            rx = recexp
            wt("scalar", "MM", cnt["MM"])
            if rrfree[0]:
                wt("scalar", "MM", rrfree[0])
            op("scalar", lambda e, pE=pE, rr=rr: e.activation(
                rr[64:65, 0:512], pE[64:65, 0:512], AF.Ln))
            op("scalar", lambda e, pO=pO, rr=rr: e.activation(
                rr[0:1, 0:512], pO[0:1, 0:512], AF.Ln), inc="SS")
            ln_mark = cnt["SS"]

            def norm_B(qt=qt, pE=pE, pO=pO, rr=rr, rx=rx, p=p, q0=q0,
                       lm=ln_mark, mprev=max(mul_mark[0], mul_mark[1])):
                wt("tensor", "SS", lm)
                op("tensor", lambda e, pO=pO, rr=rr: e.matmul(
                    pO[0:64, 0:512], onespart[64:65, 0:64],
                    rr[64:65, 0:512], start=True, stop=True))
                op("tensor", lambda e, pE=pE, rr=rr: e.matmul(
                    pE[64:128, 0:512], onespart[0:1, 0:64],
                    rr[0:1, 0:512], start=True, stop=True), inc="MM")
                rrfree[0] = cnt["MM"]
                wt("scalar", "MM", cnt["MM"])
                if mprev:
                    wt("scalar", "VV", mprev)
                op("scalar", lambda e, pO=pO, rx=rx: e.activation(
                    rx[0:64, :], pO[0:64, 0:512], AF.Exp, scale=-1.0))
                op("scalar", lambda e, pE=pE, rx=rx: e.activation(
                    rx[64:128, :], pE[64:128, 0:512], AF.Exp, scale=-1.0),
                    inc="SS")
                wt("vector", "SS", cnt["SS"])
                op("vector", lambda e, p=p, q0=q0, pE=pE, rx=rx: e.tensor_mul(
                    aout[0:64, p, q0:q0 + 512], pE[0:64, 0:512], rx[0:64, :]))
                op("vector", lambda e, p=p, q0=q0, pO=pO, rx=rx: e.tensor_mul(
                    aout[64:128, p, q0:q0 + 512], pO[64:128, 0:512],
                    rx[64:128, :]), inc="VV")
                mul_mark[qt % 2] = cnt["VV"]

            if qt < NTT - 1:
                prev_normB[0] = norm_B
            else:
                norm_pend[0] = norm_B
            if qt < NTT - 1:
                qpump(2)
                qpumpB(1)

    if norm_pend[0]:
        norm_pend[0]()
        norm_pend[0] = None
    qpump(1000, fin=True)
    barrier()
    stA1.close()
    wps = sb("wps", (128, NCH, D), BF16, stA)
    wpdma = {}
    for h_ in range(4):
        wpdma[h_] = (f"W{h_}", dma(
            "gpsimd", wps[:, :, h_ * 256:(h_ + 1) * 256],
            A(d["wproj"], h_ * 256, [[D, 128], [128 * D, NCH], [1, 256]]),
            f"W{h_}"))
    proj_mark = {0: 0, 1: 0}
    for oc in range(NCH):
        wt("tensor", *wpdma[oc // 2])
        for tt in range(NTT):
            bk = (oc * NTT + tt) % 2
            psm = [ps1, ps2][bk]
            if proj_mark[bk]:
                wt("tensor", "VV", proj_mark[bk])
            for c in range(NCH):
                op("tensor", lambda e, c=c, oc=oc, tt=tt, p2=psm: e.matmul(
                    p2[:, 0:512], wps[:, c, oc * 128:(oc + 1) * 128],
                    aout[:, c, tt * 512:(tt + 1) * 512],
                    start=(c == 0), stop=(c == NCH - 1)),
                    inc="MM" if c == NCH - 1 else None)
            wt("vector", "MM", cnt["MM"])
            op("vector", lambda e, oc=oc, tt=tt, p2=psm: e.scalar_tensor_tensor(
                xs[:, oc, tt * 512:(tt + 1) * 512], p2[:, 0:512],
                bproj_c[:, oc:oc + 1], xs[:, oc, tt * 512:(tt + 1) * 512],
                OP.add, OP.add), inc="VV")
            proj_mark[bk] = cnt["VV"]
            qpumpB(2)
    if dbg:
        wt("sync", "VV", cnt["VV"])
        dma("sync", dbgt["dbg_aout"][:, :, :], aout[:, :, :], "DI")
        dma("sync", dbgt["dbg_x1"][:, :, :], xs[:, :, :], "DI")
        wt("tensor", "DI", cnt["DI"])
    qpumpB(1000)
    barrier()
    stA.close()

    v_ln2 = layernorm()

    stQF = contextlib.ExitStack()
    B2fc = sb("B2fc", (16, FF), BF16, stQF)
    B2pj = sb("B2pj", (16, D), BF16, stQF)
    tfc = sb("tfc", (16, T), BF16, stQF)
    stQB = contextlib.ExitStack()
    Bstg = sb("Bstg", (16, FF), F32, stQB)
    Bstg2 = sb("Bstg2", (16, D), F32, stQB)
    bpj2_row = sb("bpj2_row", (1, D), F32, stQB)
    t_cst = dma("sync", bpj2_row[:, :], A(d["bpj2"], 0, [[D, 1], [1, D]]), "DI")
    wt("sync", "VV", cnt["VV"])
    t_Bfc = dma("sync", Bstg[:, :], d["BfcT"][:, :], "DI")
    t_Bpj = dma("sync", Bstg2[:, :], d["BpjT"][:, :], "DI")
    for w, (srcB, dstB, n, tB) in enumerate([(Bstg, B2fc, FF, t_Bfc),
                                             (Bstg2, B2pj, D, t_Bpj)]):
        op("vector", lambda e, w=w: e.tensor_scalar(
            qrow[0:1, 6:7], qrow[0:1, 2 * w:2 * w + 1], 2.0, None, OP.mult),
            inc="VV")
        wt("tensor", "VV", cnt["VV"])
        op("tensor", lambda e: e.matmul(ps1[0:16, 0:1], onespart[0:1, 0:16],
                                        qrow[0:1, 6:7], start=True, stop=True),
           inc="MM")
        wt("vector", "MM", cnt["MM"])
        op("vector", lambda e: e.tensor_copy(qsc[0:16, 6:7], ps1[0:16, 0:1]),
           inc="VV")
        wt("vector", "DI", cnt["DI"])
        wt("vector", "VV", cnt["VV"])
        op("vector", lambda e, srcB=srcB, dstB=dstB, n=n: e.tensor_scalar(
            dstB[:, 0:n], srcB[:, 0:n], qsc[0:16, 6:7], None, OP.mult),
            inc="VV")
    wt("vector", "DI", cnt["DI"])
    op("vector", lambda e: e.tensor_scalar(
        bpj2_rowb[:, :], bpj2_row[:, :], qrow[0:1, 2:3], None, OP.mult), inc="VV")
    v_prep = cnt["VV"]

    wt("tensor", "VV", v_ln2)
    for tt in range(NTT):
        wt("tensor", "VV", cnt["VV"])
        for c in range(NCH):
            op("tensor", lambda e, c=c, tt=tt: e.matmul(
                ps5[0:16, 0:512], afc_s[:, c, :],
                xln[:, c, tt * 512:(tt + 1) * 512],
                start=(c == 0), stop=(c == NCH - 1)),
                inc="MM" if c == NCH - 1 else None)
        wt("vector", "MM", cnt["MM"])
        op("vector", lambda e, tt=tt: e.tensor_copy(
            tfc[:, tt * 512:(tt + 1) * 512], ps5[0:16, 0:512]), inc="VV")
    wt("tensor", "VV", cnt["VV"])
    if dbg:
        wt("sync", "VV", cnt["VV"])
        dma("sync", dbgt["dbg_qsc"][:, :], qsc[:, :], "DI")
        dma("sync", dbgt["dbg_qrow"][:, :], qrow[:, :], "DI")
        dma("sync", dbgt["dbg_tfc"][:, :], tfc[:, :], "DI")
        wt("tensor", "DI", cnt["DI"])
    barrier()
    stQB.close()

    stF = contextlib.ExitStack()
    ht = sb("ht", (128, NFC, 1024), BF16, stF)
    tpj = sb("tpj", (16, 1024), BF16, stF)
    wfq = [sb(f"wfq{i}", (128, NCH, 128), BF16, stF) for i in range(2)]
    wpq = [sb(f"wpq{i}", (128, NFC, 128), BF16, stF) for i in range(2)]

    wt("scalar", "VV", v_prep)
    fcq_sem = ["DA", "DB"]
    pjq_sem = ["DC", "DD"]
    gelu_mark = {0: 0, 1: 0}
    stt_mark = {0: 0, 1: 0, 2: 0, 3: 0}
    for tt2 in range(2):
        tb0 = tt2 * 1024
        wt("scalar", "MM", cnt["MM"])
        fcq_t = {}
        wt("sync", "DI", cnt["DI"])
        wt("sync", "MM", cnt["MM"])
        for j in range(2):
            fcq_t[j] = dma("sync", wfq[j][:, :, :],
                           A(wfcq, j * 128, [[FF, 128], [128 * FF, NCH], [1, 128]]),
                           fcq_sem[j])
        for f in range(NFC):
            wt("tensor", fcq_sem[f % 2], fcq_t[f])
            for half in range(2):
                psm = [ps1, ps2][half]
                h0 = tb0 + half * 512
                wt("tensor", "SS", gelu_mark[half])
                for c in range(NCH):
                    op("tensor", lambda e, c=c, f=f, h0=h0, p2=psm: e.matmul(
                        p2[:, 0:512], wfq[f % 2][:, c, :], xln[:, c, h0:h0 + 512],
                        start=(c == 0), stop=False))
                op("tensor", lambda e, f=f, h0=h0, p2=psm: e.matmul(
                    p2[:, 0:512], B2fc[:, f * 128:(f + 1) * 128],
                    tfc[:, h0:h0 + 512], start=False, stop=True), inc="MM")
                wt("scalar", "MM", cnt["MM"])
                op("scalar", lambda e, f=f, half=half, p2=psm: e.activation(
                    ht[:, f, half * 512:(half + 1) * 512], p2[:, 0:512], AF.Gelu,
                    bias=bfc_c[:, f:f + 1], scale=qsc[:, 1:2]), inc="SS")
                gelu_mark[half] = cnt["SS"]
            if f + 2 < NFC:
                wt("sync", "MM", cnt["MM"])
                fcq_t[f + 2] = dma("sync", wfq[f % 2][:, :, :],
                                   A(wfcq, (f + 2) * 128,
                                     [[FF, 128], [128 * FF, NCH], [1, 128]]),
                                   fcq_sem[f % 2])
        wt("tensor", "SS", cnt["SS"])
        wt("tensor", "DG", cnt["DG"])
        for half in range(2):
            h0 = half * 512
            wt("tensor", "VV", cnt["VV"])
            for f in range(NFC):
                op("tensor", lambda e, f=f, h0=h0: e.matmul(
                    ps5[0:16, 0:512], apj_s[:, f, :], ht[:, f, h0:h0 + 512],
                    start=(f == 0), stop=(f == NFC - 1)),
                    inc="MM" if f == NFC - 1 else None)
            wt("vector", "MM", cnt["MM"])
            op("vector", lambda e, h0=h0: e.tensor_copy(
                tpj[:, h0:h0 + 512], ps5[0:16, 0:512]), inc="VV")
        wt("tensor", "VV", cnt["VV"])
        pjq_t = {}
        wt("sync", "DI", cnt["DI"])
        wt("sync", "MM", cnt["MM"])
        for j in range(2):
            pjq_t[j] = dma("sync", wpq[j][:, :, :],
                           A(wpqq, j * 128, [[D, 128], [128 * D, NFC], [1, 128]]),
                           pjq_sem[j])
        for oc in range(NCH):
            wt("tensor", pjq_sem[oc % 2], pjq_t[oc])
            for half in range(2):
                slot = (oc * 2 + half) % 4
                psm = [ps3, ps4][slot // 2]
                sl0 = (slot % 2) * 512
                h0 = half * 512
                wt("tensor", "VV", stt_mark[slot])
                for f in range(NFC):
                    op("tensor", lambda e, f=f, oc=oc, h0=h0, p2=psm, sl0=sl0:
                        e.matmul(p2[:, sl0:sl0 + 512], wpq[oc % 2][:, f, :],
                                 ht[:, f, h0:h0 + 512], start=(f == 0),
                                 stop=False))
                op("tensor", lambda e, oc=oc, h0=h0, p2=psm, sl0=sl0: e.matmul(
                    p2[:, sl0:sl0 + 512], B2pj[:, oc * 128:(oc + 1) * 128],
                    tpj[:, h0:h0 + 512], start=False, stop=False))
                op("tensor", lambda e, oc=oc, p2=psm, sl0=sl0: e.matmul(
                    p2[:, sl0:sl0 + 512], bpj2_rowb[0:1, oc * 128:(oc + 1) * 128],
                    onesrow512b[0:1, 0:512], start=False, stop=True), inc="MM")
                wt("vector", "MM", cnt["MM"])
                op("vector", lambda e, oc=oc, tt2=tt2, h0=h0, p2=psm, sl0=sl0:
                    e.scalar_tensor_tensor(
                        xs[:, oc, tt2 * 1024 + h0:tt2 * 1024 + h0 + 512],
                        p2[:, sl0:sl0 + 512], qsc[:, 3:4],
                        xs[:, oc, tt2 * 1024 + h0:tt2 * 1024 + h0 + 512],
                        OP.mult, OP.add), inc="VV")
                stt_mark[slot] = cnt["VV"]
            if oc + 2 < NCH:
                wt("sync", "MM", cnt["MM"])
                pjq_t[oc + 2] = dma("sync", wpq[oc % 2][:, :, :],
                                    A(wpqq, (oc + 2) * 128,
                                      [[D, 128], [128 * D, NFC], [1, 128]]),
                                    pjq_sem[oc % 2])

    wt("sync", "VV", cnt["VV"])
    t_out = dma("sync", A(yT, 0, [[T, 128], [128 * T, NCH], [1, T]]),
                xs[:, :, :], "DI")
    wt("sync", "DI", cnt["DI"])
    wt("gpsimd", "DI", cnt["DI"])
    wt("tensor", "DI", cnt["DI"])
    stF.close()
    stQF.close()

    with nc.Block() as blk:
        @blk.tensor
        def _(e):
            for fn in prog["tensor"]:
                fn(e)

        @blk.vector
        def _(e):
            for fn in prog["vector"]:
                fn(e)

        @blk.scalar
        def _(e):
            for fn in prog["scalar"]:
                fn(e)

        @blk.gpsimd
        def _(e):
            for fn in prog["gpsimd"]:
                fn(e)

        @blk.sync
        def _(e):
            for fn in prog["sync"]:
                fn(e)

    st.close()
    return nc


def _oneblk33():
    m = np.zeros((128, 33), dtype=np.float32)
    m[0:64, 0] = 1.0
    m[64:128, 32] = 1.0
    return m.astype(BF)


def _consts():
    return dict(
        tri=np.triu(np.ones((128, 128), dtype=np.float32)).astype(BF),
        iden=np.eye(128, dtype=np.float32),
        onespart=np.ones((128, 128), dtype=np.float32),
        onescolb=np.ones((1, 128), dtype=BF),
        onescol128b=np.ones((128, 1), dtype=BF),
        oneblk=_oneblk33(),
        o1024col=np.full((128, 1), 1.0 / 1024.0, dtype=np.float32),
        o1024colb=np.full((128, 1), 1.0 / 1024.0, dtype=BF),
        onesrow512b=np.ones((1, 512), dtype=BF),
    )


def kernel(L0_attn_x, L0_attn_w_qkv, L0_attn_b_qkv, L0_attn_w_proj, L0_attn_b_proj,
           L0_ffn_W_fc, L0_ffn_b_fc, L0_ffn_A_fc, L0_ffn_B_fc, L0_ffn_W_proj,
           L0_ffn_b_proj, L0_ffn_A_proj, L0_ffn_B_proj):
    x = np.asarray(L0_attn_x, dtype=np.float32)
    shared = dict(
        wqkv=np.ascontiguousarray(np.asarray(L0_attn_w_qkv, np.float32)),
        bqkv=np.ascontiguousarray(np.asarray(L0_attn_b_qkv, np.float32)),
        wproj=np.ascontiguousarray(np.asarray(L0_attn_w_proj, np.float32)),
        bproj=np.ascontiguousarray(np.asarray(L0_attn_b_proj, np.float32)),
        wfcT=np.ascontiguousarray(np.asarray(L0_ffn_W_fc, np.float32).T),
        bfc=np.ascontiguousarray(np.asarray(L0_ffn_b_fc, np.float32)),
        afcT=np.ascontiguousarray(np.asarray(L0_ffn_A_fc, np.float32).T),
        BfcT=np.ascontiguousarray(np.asarray(L0_ffn_B_fc, np.float32).T),
        wpjT=np.ascontiguousarray(np.asarray(L0_ffn_W_proj, np.float32).T),
        bpj2=np.ascontiguousarray(np.asarray(L0_ffn_b_proj, np.float32)),
        apjT=np.ascontiguousarray(np.asarray(L0_ffn_A_proj, np.float32).T),
        BpjT=np.ascontiguousarray(np.asarray(L0_ffn_B_proj, np.float32).T),
        **_consts(),
    )
    nc = build_nc()
    in_maps = []
    for c in range(B):
        m = dict(shared)
        m["xT"] = np.ascontiguousarray(x[c].T)
        in_maps.append(m)
    res = run_bass_kernel_spmd(nc, in_maps, core_ids=list(range(B)))
    out = np.stack([np.ascontiguousarray(np.asarray(r["yT"]).T)
                    for r in res.results], axis=0)
    return out.astype(np.float32)


# version 15
# speedup vs baseline: 1.0610x; 1.0610x over previous
import contextlib
import numpy as np
import ml_dtypes
import concourse.bass as bass
import concourse.mybir as mybir
from concourse.bass_utils import run_bass_kernel_spmd

F32 = mybir.dt.float32
BF16 = mybir.dt.bfloat16
I32 = mybir.dt.int32
AF = mybir.ActivationFunctionType
OP = mybir.AluOpType
AX = mybir.AxisListType
BF = ml_dtypes.bfloat16

B, T, D, H = 8, 2048, 1024, 16
NCH = D // 128
NTT = T // 512
FF = 4096
NFC = FF // 128
EPS = 1e-5


def build_nc(dbg=False):
    nc = bass.Bass("TRN2", target_bir_lowering=False)
    d = {}

    def din(name, shape, dt):
        d[name] = nc.dram_tensor(name, list(shape), dt, kind="ExternalInput")

    din("xT", (D, T), F32)
    din("wqkv", (D, 3 * D), F32)
    din("bqkv", (3 * D,), F32)
    din("wproj", (D, D), F32)
    din("bproj", (D,), F32)
    din("wfcT", (D, FF), F32)
    din("bfc", (FF,), F32)
    din("afcT", (D, 16), F32)
    din("BfcT", (16, FF), F32)
    din("wpjT", (FF, D), F32)
    din("bpj2", (D,), F32)
    din("apjT", (FF, 16), F32)
    din("BpjT", (16, D), F32)
    din("tri", (128, 128), BF16)
    din("iden", (128, 128), F32)
    din("onespart", (128, 128), F32)
    din("onescolb", (1, 128), BF16)
    din("onescol128b", (128, 1), BF16)
    din("oneblk", (128, 33), BF16)
    din("o1024col", (128, 1), F32)
    din("o1024colb", (128, 1), BF16)
    din("onesrow512b", (1, 512), BF16)
    yT = nc.dram_tensor("yT", [D, T], F32, kind="ExternalOutput")
    dbgt = {}
    if dbg:
        for nm, shape, dt in [("dbg_xln", (128, NCH, T), BF16),
                              ("dbg_q", (128, T), BF16), ("dbg_k", (128, T), BF16),
                              ("dbg_vt", (128, 16, 128), BF16),
                              ("dbg_bias", (128, 2), F32),
                              ("dbg_pb", (128, 1024), BF16),
                              ("dbg_aout", (128, NCH, T), BF16),
                              ("dbg_x1", (128, NCH, T), F32),
                              ("dbg_qsc", (128, 8), F32),
                              ("dbg_qrow", (1, 8), F32),
                              ("dbg_tfc", (16, T), BF16),
                              ("dbg_ht", (128, NFC, 1024), BF16),
                              ("dbg_rowm", (1, T), F32),
                              ("dbg_rowq2", (1, T), F32),
                              ("dbg_rowrstd", (1, T), F32),
                              ("dbg_mh", (128, T), F32),
                              ("dbg_sh", (128, T), F32),
                              ("dbg_wtl", (128, NCH, 3, 128), BF16),
                              ("dbg_brow", (1, 3 * D), BF16)]:
            dbgt[nm] = nc.dram_tensor(nm, list(shape), dt, kind="ExternalOutput")
    wfcq = nc.dram_tensor("wfcq_scratch", [D, FF], BF16)
    wpqq = nc.dram_tensor("wpq_scratch", [FF, D], BF16)

    st = contextlib.ExitStack()
    sb = lambda n, s, t, stk=None: (stk or st).enter_context(
        nc.sbuf_tensor(n, list(s), t))
    A = bass.AP

    xs = sb("xs", (128, NCH, T), F32)
    xln = sb("xln", (128, NCH, T), BF16)
    sqtmp = sb("sqtmp", (128, 1024), BF16)
    sqtmp2 = sb("sqtmp2", (128, 1024), BF16)
    tri = sb("z_tri", (128, 128), BF16)
    iden = sb("z_iden", (128, 128), F32)
    onespart = sb("z_onespart", (128, 128), F32)
    onescolb = sb("z_onescolb", (1, 128), BF16)
    onescol128b = sb("z_onescol128b", (128, 1), BF16)
    oneblk = sb("z_oneblk", (128, 33), BF16)
    o1024col = sb("z_o1024col", (128, 1), F32)
    o1024colb = sb("z_o1024colb", (128, 1), BF16)
    onesrow512b = sb("z_onesrow512b", (1, 512), BF16)
    bqkv_c = sb("bqkv_c", (128, 24), F32)
    bproj_c = sb("bproj_c", (128, 8), F32)
    bfc_c = sb("bfc_c", (128, 32), F32)
    bpj2_rowb = sb("bpj2_rowb", (1, D), BF16)
    afc_s = sb("afc_s", (128, NCH, 16), BF16)
    apj_s = sb("apj_s", (128, NFC, 16), BF16)
    nrm = sb("nrm", (33, 8), F32)
    nrmt = sb("nrmt", (33, 12), F32)
    biasEO = sb("biasEO", (128, 2), F32)
    qsc = sb("qsc", (128, 8), F32)
    qrow = sb("qrow", (1, 8), F32)
    barscr = sb("barscr", (1, 8), F32)
    fillscr = sb("fillscr", (1, 16), F32)

    ps1 = st.enter_context(nc.psum_tensor("ps1", [128, 512], F32))
    ps2 = st.enter_context(nc.psum_tensor("ps2", [128, 512], F32))
    ps3 = st.enter_context(nc.psum_tensor("ps3", [128, 1024], F32))
    ps4 = st.enter_context(nc.psum_tensor("ps4", [128, 1024], F32))
    ps5 = st.enter_context(nc.psum_tensor("ps5", [128, 512], F32))
    ps6 = st.enter_context(nc.psum_tensor("ps6", [128, 512], F32))

    sems = {}
    cnt = {}
    for s in ("DI", "DG", "MM", "VV", "SS", "BR", "GG", "DA", "DB", "DC", "DD",
              "X0", "X1", "X2", "X3", "X4", "X5", "X6", "X7",
              "W0", "W1", "W2", "W3"):
        sems[s] = st.enter_context(nc.semaphore(s))
        cnt[s] = 0
    ENGS = ("tensor", "vector", "scalar", "gpsimd", "sync")
    prog = {k: [] for k in ENGS}

    def op(eng, fn, inc=None, n=1):
        tgt = None
        if inc is not None:
            cnt[inc] += n
            tgt = cnt[inc]

            def thunk(e, fn=fn, s=inc, n=n):
                fn(e).then_inc(sems[s], n)
        else:
            def thunk(e, fn=fn):
                fn(e)
        prog[eng].append(thunk)
        return tgt

    def wt(eng, sem, tgt):
        prog[eng].append(lambda e, s=sem, t=tgt: e.wait_ge(sems[s], t))

    def bump(eng, sem):
        cnt[sem] += 1
        tgt = cnt[sem]
        prog[eng].append(lambda e, s=sem: e.nop().then_inc(sems[s], 1))
        return tgt

    def dma(eng, dst, src, sem, slow=False):
        cnt[sem] += 16
        tgt = cnt[sem]
        prog[eng].append(
            lambda e, d2=dst, s2=src, s=sem, sl=slow: e.dma_start(
                d2, s2, allow_slow_non_contiguous=sl).then_inc(sems[s], 16))
        return tgt

    nbar = [0]
    nfil = [0]

    def vfill(n=4):
        for _ in range(n):
            nfil[0] += 1
            sl = nfil[0] % 8
            prog["vector"].append(
                lambda e, sl=sl: e.tensor_copy(fillscr[0:1, 8 + sl:9 + sl],
                                               fillscr[0:1, sl:sl + 1]))

    def barrier():
        nbar[0] += 1
        tgt = 5 * nbar[0]
        cnt["BR"] = tgt
        for eng in ENGS:
            for s_ in ("DI", "DG", "GG", "DA", "DB", "DC", "DD",
                       "X0", "X1", "X2", "X3", "X4", "X5", "X6", "X7",
                       "W0", "W1", "W2", "W3"):
                wt(eng, s_, cnt[s_])
        wt("tensor", "MM", cnt["MM"])
        wt("tensor", "SS", cnt["SS"])
        prog["tensor"].append(lambda e: e.nop().then_inc(sems["BR"], 1))
        prog["sync"].append(lambda e: e.nop().then_inc(sems["BR"], 1))
        prog["vector"].append(
            lambda e: e.memset(barscr[0:1, 0:1], 0.0).then_inc(sems["BR"], 1))
        prog["scalar"].append(
            lambda e: e.memzero(barscr[0:1, 2:3]).then_inc(sems["BR"], 1))
        prog["gpsimd"].append(
            lambda e: e.tensor_copy(barscr[0:1, 4:5], barscr[0:1, 5:6]
                                    ).then_inc(sems["BR"], 1))
        for eng in ENGS:
            wt(eng, "BR", tgt)

    prog["vector"].append(lambda e: e.memset(fillscr[0:1, 0:8], 0.0))
    prog["gpsimd"].append(lambda e: e.memset(barscr[0:1, 5:6], 0.0))
    for name, sbt in [("tri", tri), ("iden", iden), ("onespart", onespart),
                      ("onescolb", onescolb), ("oneblk", oneblk),
                      ("o1024col", o1024col), ("o1024colb", o1024colb),
                      ("onesrow512b", onesrow512b), ("onescol128b", onescol128b)]:
        dma("sync", sbt[:, :], d[name][:, :], "DI")
    xdma = {}
    for tt in range(NTT):
        for par in range(2):
            s_ = f"X{tt * 2 + par}"
            xdma[(tt, par)] = (s_, dma(
                ["sync", "scalar"][par],
                A(xs, par * T + tt * 512, [[NCH * T, 128], [2 * T, 4], [1, 512]]),
                A(d["xT"], par * 128 * T + tt * 512,
                  [[T, 128], [2 * 128 * T, 4], [1, 512]]),
                s_))
    dma("sync", bqkv_c[:, :], A(d["bqkv"], 0, [[1, 128], [128, 24]]), "GG", slow=True)
    dma("sync", bproj_c[:, :], A(d["bproj"], 0, [[1, 128], [128, 8]]), "GG", slow=True)
    dma("sync", bfc_c[:, :], A(d["bfc"], 0, [[1, 128], [128, 32]]), "GG", slow=True)
    dma("gpsimd", afc_s[:, :, :],
        A(d["afcT"], 0, [[16, 128], [128 * 16, NCH], [1, 16]]), "DG")
    t_apj = dma("gpsimd", apj_s[:, :, :],
                A(d["apjT"], 0, [[16, 128], [128 * 16, NFC], [1, 16]]), "DG")

    ln_idx = [0]

    def layernorm():
        ln_idx[0] += 1
        sfx = str(ln_idx[0])
        stL = contextlib.ExitStack()
        Mh = sb("Mh" + sfx, (128, T), F32, stL)
        Sh = sb("Sh" + sfx, (128, T), F32, stL)
        rowm = sb("rowm" + sfx, (1, T), F32, stL)
        rowq2 = sb("rowq2" + sfx, (1, T), F32, stL)
        rowvar = sb("rowvar" + sfx, (1, T), F32, stL)
        rowln = sb("rowln" + sfx, (1, T), F32, stL)
        rowrstd = sb("rowrstd" + sfx, (1, T), F32, stL)
        lnscr = sb("lnscrL" + sfx, (128, 2048), F32, stL)
        wt("tensor", "DI", cnt["DI"])
        wt("tensor", "VV", cnt["VV"])
        wt("scalar", "DI", cnt["DI"])
        wt("scalar", "VV", cnt["VV"])
        wt("vector", "SS", cnt["SS"])
        slotmm = {0: 0, 1: 0}
        rowevac_mark = {0: 0, 1: 0}
        expevac_mark = {0: 0, 1: 0}
        first_ln = (ln_idx[0] == 1)
        for tt in range(NTT):
            psmA = [ps1, ps2][tt % 2]
            psmB = [ps5, ps6][tt % 2]
            if tt >= 2:
                wt("tensor", "VV", rowevac_mark[tt % 2])
            for c in range(NCH):
                sl = (c % 2) * 512
                eng = ["scalar", "vector"][c % 2]
                sem = ["SS", "VV"][c % 2]
                if slotmm[c % 2]:
                    wt(eng, "MM", slotmm[c % 2])
                if first_ln:
                    sx, vx = xdma[(tt, c % 2)]
                    wt(eng, sx, vx)
                if eng == "scalar":
                    op("scalar", lambda e, c=c, tt=tt, sl=sl: e.activation(
                        sqtmp[:, sl:sl + 512], xs[:, c, tt * 512:(tt + 1) * 512],
                        AF.Copy))
                    op("scalar", lambda e, c=c, tt=tt, sl=sl: e.activation(
                        sqtmp2[:, sl:sl + 512], xs[:, c, tt * 512:(tt + 1) * 512],
                        AF.Square), inc="SS")
                else:
                    op("vector", lambda e, c=c, tt=tt, sl=sl: e.tensor_copy(
                        sqtmp[:, sl:sl + 512], xs[:, c, tt * 512:(tt + 1) * 512]))
                    op("vector", lambda e, c=c, tt=tt, sl=sl: e.tensor_tensor(
                        sqtmp2[:, sl:sl + 512], xs[:, c, tt * 512:(tt + 1) * 512],
                        xs[:, c, tt * 512:(tt + 1) * 512], OP.mult), inc="VV")
                wt("tensor", sem, cnt[sem])
                op("tensor", lambda e, c=c, p=psmA, sl=sl: e.matmul(
                    p[0:1, 0:512], o1024colb[:, 0:1], sqtmp[:, sl:sl + 512],
                    start=(c == 0), stop=(c == NCH - 1)))
                op("tensor", lambda e, c=c, p=psmB, sl=sl: e.matmul(
                    p[0:1, 0:512], o1024colb[:, 0:1], sqtmp2[:, sl:sl + 512],
                    start=(c == 0), stop=(c == NCH - 1)), inc="MM")
                slotmm[c % 2] = cnt["MM"]
            ts = tt * 512
            wt("vector", "MM", cnt["MM"])
            op("vector", lambda e, p=psmA, ts=ts: e.tensor_copy(
                rowm[0:1, ts:ts + 512], p[0:1, 0:512]))
            op("vector", lambda e, p=psmB, ts=ts: e.tensor_copy(
                rowq2[0:1, ts:ts + 512], p[0:1, 0:512]), inc="VV")
            rowevac_mark[tt % 2] = cnt["VV"]
            wt("vector", "VV", cnt["VV"])
            op("vector", lambda e, ts=ts: e.tensor_mul(
                rowvar[0:1, ts:ts + 512], rowm[0:1, ts:ts + 512],
                rowm[0:1, ts:ts + 512]), inc="VV")
            wt("vector", "VV", cnt["VV"])
            op("vector", lambda e, ts=ts: e.scalar_tensor_tensor(
                rowvar[0:1, ts:ts + 512], rowq2[0:1, ts:ts + 512], EPS,
                rowvar[0:1, ts:ts + 512], OP.add, OP.subtract), inc="VV")
            wt("scalar", "VV", cnt["VV"])
            op("scalar", lambda e, ts=ts: e.activation(
                rowln[0:1, ts:ts + 512], rowvar[0:1, ts:ts + 512], AF.Ln),
                inc="SS")
            wt("scalar", "SS", cnt["SS"])
            op("scalar", lambda e, ts=ts: e.activation(
                rowrstd[0:1, ts:ts + 512], rowln[0:1, ts:ts + 512], AF.Exp,
                scale=-0.5), inc="SS")
            eb = [ps3, ps4][tt % 2]
            wt("tensor", "SS", cnt["SS"])
            if expevac_mark[tt % 2]:
                wt("tensor", "VV", expevac_mark[tt % 2])
            op("tensor", lambda e, eb=eb, ts=ts: e.matmul(
                eb[:, 0:512], onespart[0:1, :], rowm[0:1, ts:ts + 512],
                start=True, stop=True))
            op("tensor", lambda e, eb=eb, ts=ts: e.matmul(
                eb[:, 512:1024], onespart[0:1, :], rowrstd[0:1, ts:ts + 512],
                start=True, stop=True), inc="MM")
            wt("vector", "MM", cnt["MM"])
            op("vector", lambda e, eb=eb, ts=ts: e.tensor_copy(
                Mh[:, ts:ts + 512], eb[:, 0:512]))
            op("vector", lambda e, eb=eb, ts=ts: e.tensor_copy(
                Sh[:, ts:ts + 512], eb[:, 512:1024]), inc="VV")
            expevac_mark[tt % 2] = cnt["VV"]
        if first_ln:
            for k_ in range(8):
                wt("vector", f"X{k_}", cnt[f"X{k_}"])
        if dbg and ln_idx[0] == 1:
            wt("sync", "VV", cnt["VV"])
            dma("sync", dbgt["dbg_rowm"][:, :], rowm[:, :], "DI")
            dma("sync", dbgt["dbg_rowq2"][:, :], rowq2[:, :], "DI")
            dma("sync", dbgt["dbg_rowrstd"][:, :], rowrstd[:, :], "DI")
            dma("sync", dbgt["dbg_mh"][:, :], Mh[:, :], "DI")
            dma("sync", dbgt["dbg_sh"][:, :], Sh[:, :], "DI")
            wt("vector", "DI", cnt["DI"])
        pairs = [(c, tt) for c in range(NCH) for tt in range(NTT)]
        for g in range(0, len(pairs), 4):
            grp = pairs[g:g + 4]
            wt("vector", "VV", cnt["VV"])
            for i, (c, tt) in enumerate(grp):
                op("vector", lambda e, c=c, tt=tt, i=i: e.tensor_sub(
                    lnscr[:, i * 512:(i + 1) * 512],
                    xs[:, c, tt * 512:(tt + 1) * 512],
                    Mh[:, tt * 512:(tt + 1) * 512]),
                    inc="VV" if (c, tt) == grp[-1] else None)
            wt("vector", "VV", cnt["VV"])
            for i, (c, tt) in enumerate(grp):
                op("vector", lambda e, c=c, tt=tt, i=i: e.tensor_mul(
                    xln[:, c, tt * 512:(tt + 1) * 512],
                    lnscr[:, i * 512:(i + 1) * 512],
                    Sh[:, tt * 512:(tt + 1) * 512]),
                    inc="VV" if (c, tt) == grp[-1] else None)
        v = cnt["VV"]
        barrier()
        stL.close()
        return v

    v_ln1 = layernorm()
    if dbg:
        wt("sync", "VV", v_ln1)
        dma("sync", dbgt["dbg_xln"][:, :, :], xln[:, :, :], "DI")
        wt("tensor", "DI", cnt["DI"])

    stA = contextlib.ExitStack()
    stA1 = contextlib.ExitStack()
    aout = sb("aout", (128, NCH, T), BF16, stA)
    stg = sb("stg", (128, 1024), F32, stA)
    stg2 = sb("stg2", (128, 1024), F32, stA)
    bqkv_rowb = sb("bqkv_rowb", (1, 3 * D), BF16, stA)
    stgB = sb("stgB", (128, 1024), F32, stA)
    stgB2 = sb("stgB2", (128, 1024), F32, stA)
    qi = sb("qi", (128, 1024), I32, stA)
    qb = sb("qb", (128, 1024), BF16, stA)
    wmax32 = sb("wmax32", (128, 64), F32, stA)
    srow128 = sb("srow128", (1, 128), F32, stA)
    Qp = sb("Qp", (128, T), BF16, stA1)
    Kp = sb("Kp", (128, T), BF16, stA1)
    Vt = sb("Vt", (128, 16, 132), BF16, stA1)
    op("vector", lambda e: e.memset(Vt[:, :, 64:66], 1.0))
    op("vector", lambda e: e.memset(Vt[:, :, 130:132], 1.0), inc="VV")
    wtl = [sb(f"wtl{i}", (128, NCH, 3, 128), BF16, stA1) for i in range(2)]
    PB = [sb(f"PB{i}", (128, 1024), BF16, stA1) for i in range(2)]
    recrow = sb("recrow", (128, 512), F32, stA1)
    recexp = sb("recexp", (128, 512), F32, stA1)
    t_brow = dma("gpsimd", bqkv_rowb[:, :], A(d["bqkv"], 0, [[3 * D, 1], [1, 3 * D]]),
                 "DG")

    def dma_wtl(p):
        buf = wtl[p % 2]
        t = None
        for j, coff in enumerate([128 * p, D + 128 * p, 2 * D + 128 * p]):
            t = dma("gpsimd", buf[:, :, j, :],
                    A(d["wqkv"], coff, [[3 * D, 128], [128 * 3 * D, NCH], [1, 128]]),
                    "DG")
        return t

    qsc_mark = {0: 0, 1: 0}
    QCH = 1024
    qfill_sem = ["DA", "DB"]
    qfillB_sem = ["DC", "DD"]

    def quant_passA_units(w, src_name, nrows, inner):
        nfill = nrows * inner // (128 * QCH)
        per_stage = inner // QCH
        def ap_of(f):
            g = f // per_stage
            cc = f % per_stage
            return A(d[src_name], g * 128 * inner + cc * QCH,
                     [[inner, 128], [1, QCH]])
        fills = {}
        marks = {}

        def fill(f):
            if f < 2:
                wt("sync", "VV", cnt["VV"])
            else:
                wt("sync", "VV", marks[f - 2])
            fills[f] = dma("sync", [stg, stg2][f % 2][:, :], ap_of(f),
                           qfill_sem[f % 2])
        def prime():
            fill(0)
            fill(1)
        units = []
        for f in range(nfill):
            def unit(f=f):
                s_ = [stg, stg2][f % 2]
                wt("vector", qfill_sem[f % 2], fills[f])
                if f < 2:
                    wt("vector", "VV", cnt["VV"])
                op("vector", lambda e, s_=s_, f=f: e.tensor_reduce(
                    wmax32[:, f:f + 1], s_[:, :], AX.X, OP.max,
                    apply_absolute_value=True), inc="VV")
                marks[f] = cnt["VV"]
                if f + 2 < nfill:
                    fill(f + 2)
            units.append(unit)
        return prime, units

    def quant_finalize(w, nfill):
        wt("vector", "VV", cnt["VV"])
        width = nfill
        srcbase = 0
        dstbase = 32
        while width > 1:
            half = (width + 1) // 2
            for j in range(half):
                a_ = srcbase + 2 * j
                b_ = min(srcbase + 2 * j + 1, srcbase + width - 1)
                op("vector", lambda e, a_=a_, b_=b_, j=j, dstbase=dstbase:
                    e.tensor_tensor(wmax32[:, dstbase + j:dstbase + j + 1],
                                    wmax32[:, a_:a_ + 1], wmax32[:, b_:b_ + 1],
                                    OP.max),
                    inc="VV" if j == half - 1 else None)
            wt("vector", "VV", cnt["VV"])
            srcbase, dstbase = dstbase, (0 if dstbase == 32 else 32)
            width = half
        fin = srcbase
        wt("tensor", "VV", cnt["VV"])
        op("tensor", lambda e, fin=fin: e.transpose(
            ps3[0:1, 0:128], wmax32[:, fin:fin + 1], iden[:, :]), inc="MM")
        wt("vector", "MM", cnt["MM"])
        op("vector", lambda e: e.tensor_copy(srow128[0:1, :], ps3[0:1, 0:128]),
           inc="VV")
        wt("vector", "VV", cnt["VV"])
        op("vector", lambda e, w=w: e.tensor_reduce(
            qrow[0:1, 2 * w:2 * w + 1], srow128[0:1, :], AX.X, OP.max), inc="VV")
        wt("vector", "VV", cnt["VV"])
        op("vector", lambda e, w=w: e.reciprocal(qrow[0:1, 4:5],
                                                 qrow[0:1, 2 * w:2 * w + 1]))
        op("vector", lambda e, w=w: e.tensor_scalar(
            qrow[0:1, 2 * w + 1:2 * w + 2], qrow[0:1, 2 * w:2 * w + 1],
            1.0 / 127.0, None, OP.mult), inc="VV")
        wt("vector", "VV", cnt["VV"])
        op("vector", lambda e, w=w: e.tensor_scalar(
            qrow[0:1, 2 * w:2 * w + 1], qrow[0:1, 4:5], 127.0, None, OP.mult),
            inc="VV")
        wt("tensor", "VV", cnt["VV"])
        op("tensor", lambda e, w=w: e.matmul(
            ps3[:, 0:1], onespart[0:1, :], qrow[0:1, 2 * w:2 * w + 1],
            start=True, stop=True))
        op("tensor", lambda e, w=w: e.matmul(
            ps3[:, 1:2], onespart[0:1, :], qrow[0:1, 2 * w + 1:2 * w + 2],
            start=True, stop=True), inc="MM")
        wt("vector", "MM", cnt["MM"])
        op("vector", lambda e, w=w: e.tensor_copy(qsc[:, 2 * w:2 * w + 2],
                                                  ps3[:, 0:2]), inc="VV")
        qsc_mark[w] = cnt["VV"]
        wt("tensor", "VV", cnt["VV"])

    def quant_passB_units(w, src_name, nrows, inner, scr):
        nfill = nrows * inner // (128 * QCH)
        per_stage = inner // QCH
        def ap_of(f, dr=False):
            g = f // per_stage
            cc = f % per_stage
            t = scr if dr else d[src_name]
            return A(t, g * 128 * inner + cc * QCH, [[inner, 128], [1, QCH]])
        fills = {}
        marks = {}

        def fill(f):
            if f < 2:
                wt("sync", "SS", cnt["SS"])
                wt("sync", "VV", cnt["VV"])
            else:
                wt("sync", "SS", marks[f - 2])
            fills[f] = dma("sync", [stgB, stgB2][f % 2][:, :], ap_of(f),
                           qfillB_sem[f % 2])
        def prime():
            fill(0)
            fill(1)
        units = []
        for f in range(nfill):
            def unit(f=f):
                s_ = [stgB, stgB2][f % 2]
                wt("scalar", qfillB_sem[f % 2], fills[f])
                wt("scalar", "DI", cnt["DI"])
                wt("scalar", "SS", cnt["SS"])
                wt("scalar", "VV", qsc_mark[w])
                op("scalar", lambda e, s_=s_, w=w: e.activation(
                    qi[:, 0:1024], s_[:, 0:1024], AF.Copy,
                    scale=qsc[:, 2 * w:2 * w + 1]), inc="SS")
                wt("scalar", "SS", cnt["SS"])
                op("scalar", lambda e: e.activation(
                    qb[:, 0:1024], qi[:, 0:1024], AF.Copy), inc="SS")
                marks[f] = cnt["SS"]
                wt("sync", "SS", cnt["SS"])
                dma("sync", ap_of(f, dr=True), qb[:, :], "DI")
                if f + 2 < nfill:
                    fill(f + 2)
            units.append(unit)
        return prime, units

    _pj_A_prime, _pj_A = quant_passA_units(1, "wpjT", NFC * 128, D)
    _fc_A_prime, _fc_A = quant_passA_units(0, "wfcT", NCH * 128, FF)
    _fc_B_prime, _fc_B = quant_passB_units(0, "wfcT", NCH * 128, FF, wfcq)
    _pj_B_prime, _pj_B = quant_passB_units(1, "wpjT", NFC * 128, D, wpqq)
    qworkA = (_fc_A + ["FIN_FC"] + _pj_A + ["FIN_PJ"])
    qworkB = (_fc_B + _pj_B)
    qposA = [0]
    qposB = [0]

    def qpump(n, fin=False):
        while n > 0 and qposA[0] < len(qworkA):
            u = qworkA[qposA[0]]
            if u == "FIN_PJ" or u == "FIN_FC":
                if not fin:
                    return
                qposA[0] += 1
                quant_finalize(1 if u == "FIN_PJ" else 0, 32)
                if u == "FIN_FC":
                    _pj_A_prime()
            else:
                qposA[0] += 1
                u()
                n -= 1

    def qpumpB(n):
        while n > 0 and qposB[0] < len(qworkB):
            w_ = 0 if qposB[0] < len(_fc_B) else 1
            if qsc_mark[w_] == 0:
                return
            if qposB[0] == len(_fc_B):
                _pj_B_prime()
            u = qworkB[qposB[0]]
            qposB[0] += 1
            u()
            n -= 1

    wtl_t = {0: dma_wtl(0)}
    _fc_A_prime()
    _fc_B_prime()
    av_mm_for_buf = {0: 0, 1: 0}
    mul_mark = {0: 0, 1: 0}
    qk_mark = {0: 0, 1: 0}
    v_mark = {0: 0, 1: 0}
    exp_hist = [0]
    csmm_mark = {}
    red_mark = {0: 0, 1: 0}
    norm_pend = [None]
    biasEO_mark = [0]
    rrfree = {0: 0, 1: 0}

    if dbg:
        wt("sync", "DG", cnt["DG"])
        dma("sync", dbgt["dbg_wtl"][:, :, :, :], wtl[0][:, :, :, :], "DI")
        wt("sync", "DG", cnt["DG"])
        dma("sync", dbgt["dbg_brow"][:, :], bqkv_rowb[:, :], "DI")
        wt("tensor", "DI", cnt["DI"])
    for p in range(H // 2):
        buf = p % 2
        wt("tensor", "DG", cnt["DG"])
        wt("tensor", "DG", cnt["DG"])
        wt("tensor", "VV", v_ln1)
        qpump(1, fin=True)
        qpumpB(2)
        def emit_cs(w, tt, buf=buf):
            csb = [ps3, ps4][tt % 2]
            sq_ = [sqtmp, sqtmp2][tt // 2]
            sl = (tt % 2) * 512
            wt("tensor", "SS", sq_mark[(w, tt // 2)])
            if red_mark[tt % 2]:
                wt("tensor", "VV", red_mark[tt % 2])
            if exp_hist[0]:
                wt("tensor", "SS", exp_hist[0])
            op("tensor", lambda e, csb=csb, sq_=sq_, sl=sl: e.matmul(
                csb[0:1, 0:512], onescol128b[:, 0:1], sq_[:, sl:sl + 512],
                start=True, stop=True), inc="MM")
            csmm_mark[(w, tt)] = cnt["MM"]
            wt("vector", "MM", cnt["MM"])
            op("vector", lambda e, w=w, tt=tt, csb=csb: e.tensor_reduce(
                nrmt[0:1, 4 * w + tt:4 * w + tt + 1], csb[0:1, 0:512],
                AX.X, OP.max), inc="VV")
            red_mark[tt % 2] = cnt["VV"]

        sq_mark = {}

        def emit_group(w, tt, buf=buf):
            dst = [Qp, Kp][w]
            boff = 128 * p + w * D
            psm = [ps1, ps2][tt % 2]
            wt("tensor", "VV", qk_mark[tt % 2])
            for c in range(NCH):
                op("tensor", lambda e, c=c, tt=tt, p2=psm, w=w, buf=buf:
                    e.matmul(
                        p2[:, 0:512], wtl[buf][:, c, w, :],
                        xln[:, c, tt * 512:(tt + 1) * 512],
                        start=(c == 0), stop=False))
            op("tensor", lambda e, p2=psm, bo=boff: e.matmul(
                p2[:, 0:512], bqkv_rowb[0:1, bo:bo + 128],
                onesrow512b[0:1, 0:512], start=False, stop=True), inc="MM")
            wt("vector", "MM", cnt["MM"])
            op("vector", lambda e, tt=tt, p2=psm, dst=dst: e.tensor_copy(
                dst[:, tt * 512:(tt + 1) * 512], p2[:, 0:512]), inc="VV")
            qk_mark[tt % 2] = cnt["VV"]
            if tt % 2 == 1:
                half = tt // 2
                sq_ = [sqtmp, sqtmp2][half]
                wt("scalar", "VV", qk_mark[1])
                wt("scalar", "MM", csmm_mark.get((1 - w, 2 * half + 1), 0))
                op("scalar", lambda e, dst=dst, half=half, sq_=sq_:
                    e.activation(sq_[:, 0:1024],
                                 dst[:, half * 1024:(half + 1) * 1024],
                                 AF.Square), inc="SS")
                sq_mark[(w, half)] = cnt["SS"]

        emit_group(0, 0)
        emit_group(0, 1)
        emit_group(0, 2)
        emit_cs(0, 0)
        emit_group(0, 3)
        emit_cs(0, 1)
        emit_group(1, 0)
        emit_cs(0, 2)
        emit_group(1, 1)
        emit_cs(0, 3)
        emit_group(1, 2)
        emit_cs(1, 0)
        emit_group(1, 3)
        emit_cs(1, 1)
        if norm_pend[0]:
            norm_pend[0]()
            norm_pend[0] = None
        qpumpB(2)
        wt("vector", "VV", cnt["VV"])
        wt("tensor", "DG", cnt["DG"])
        wt("tensor", "VV", mul_mark[1])

        def emit_v(tc, buf=buf):
            psm = [ps5, ps6][tc % 2]
            wt("tensor", "VV", v_mark[tc % 2])
            for c in range(NCH):
                op("tensor", lambda e, c=c, tc=tc, p2=psm, buf=buf: e.matmul(
                    p2[:, 0:128], xln[:, c, tc * 128:(tc + 1) * 128],
                    wtl[buf][:, c, 2, :], start=(c == 0), stop=False))
            op("tensor", lambda e, tc=tc, p2=psm, p=p: e.matmul(
                p2[:, 0:128], onescolb[:, :],
                bqkv_rowb[0:1, 2 * D + 128 * p:2 * D + 128 * p + 128],
                start=False, stop=True), inc="MM")
            wt("vector", "MM", cnt["MM"])
            op("vector", lambda e, tc=tc, p2=psm, buf=buf: e.tensor_copy(
                A(Vt, tc * 132, [[16 * 132, 128], [66, 2], [1, 64]]),
                A(p2, 0, [[512, 128], [64, 2], [1, 64]])), inc="VV")
            v_mark[tc % 2] = cnt["VV"]

        emit_v(0)
        emit_cs(1, 2)
        emit_v(1)
        emit_cs(1, 3)
        emit_v(2)
        emit_v(3)
        wt("vector", "VV", cnt["VV"])
        for w in range(2):
            op("vector", lambda e, w=w: e.tensor_tensor(
                nrmt[0:1, 8 + 2 * w:9 + 2 * w], nrmt[0:1, 4 * w:4 * w + 1],
                nrmt[0:1, 4 * w + 1:4 * w + 2], OP.max))
            op("vector", lambda e, w=w: e.tensor_tensor(
                nrmt[0:1, 9 + 2 * w:10 + 2 * w], nrmt[0:1, 4 * w + 2:4 * w + 3],
                nrmt[0:1, 4 * w + 3:4 * w + 4], OP.max),
                inc="VV" if w == 1 else None)
        wt("vector", "VV", cnt["VV"])
        for w in range(2):
            op("vector", lambda e, w=w: e.tensor_tensor(
                nrm[0:1, w:w + 1], nrmt[0:1, 8 + 2 * w:9 + 2 * w],
                nrmt[0:1, 9 + 2 * w:10 + 2 * w], OP.max),
                inc="VV" if w == 1 else None)
        wt("scalar", "VV", cnt["VV"])
        op("scalar", lambda e: e.activation(nrm[0:1, 4:6], nrm[0:1, 0:2], AF.Sqrt),
           inc="SS")
        wt("vector", "SS", cnt["SS"])
        op("vector", lambda e: e.tensor_mul(nrm[0:1, 6:7], nrm[0:1, 4:5],
                                            nrm[0:1, 5:6]), inc="VV")
        wt("vector", "VV", cnt["VV"])
        op("vector", lambda e: e.tensor_scalar(nrm[0:1, 7:8], nrm[0:1, 6:7],
                                               -0.125, None, OP.mult), inc="VV")
        emit_v(4)
        wt("tensor", "VV", cnt["VV"])
        op("tensor", lambda e: e.matmul(ps1[:, 0:1], onespart[0:1, :],
                                        nrm[0:1, 7:8], start=True, stop=True),
           inc="MM")
        wt("vector", "MM", cnt["MM"])
        if exp_hist[0]:
            wt("vector", "SS", exp_hist[0])
        op("vector", lambda e: e.tensor_copy(biasEO[:, 0:1], ps1[:, 0:1]), inc="VV")
        biasEO_mark[0] = cnt["VV"]
        for tc in range(5, 16):
            emit_v(tc)
        if p + 1 < H // 2:
            wt("gpsimd", "MM", cnt["MM"])
            wtl_t[p + 1] = dma_wtl(p + 1)
        prev_normB = [None]
        for qt in range(NTT):
            kmax = 4 * (qt + 1)
            q0 = qt * 512
            pE = [ps1, ps5][qt % 2]
            pO = [ps2, ps6][qt % 2]
            mm_of = {}
            pbr_of = {}
            exp_of = {}

            def emit_phase2(kc):
                psS = [ps3, ps4][kc % 2]
                if qt == 0 and kc < 2 and red_mark[kc % 2]:
                    wt("tensor", "VV", red_mark[kc % 2])
                if kc >= 2:
                    wt("tensor", "SS", exp_of[kc - 2])
                elif exp_hist[0]:
                    wt("tensor", "SS", exp_hist[0])
                op("tensor", lambda e, kc=kc, buf=buf, q0=q0, pS=psS: e.matmul(
                    pS[:, 0:512], Kp[0:64, kc * 128:(kc + 1) * 128],
                    Qp[0:64, q0:q0 + 512], start=True, stop=True))
                op("tensor", lambda e, kc=kc, buf=buf, q0=q0, pS=psS: e.matmul(
                    pS[:, 512:1024], Kp[64:128, kc * 128:(kc + 1) * 128],
                    Qp[64:128, q0:q0 + 512], start=True, stop=True), inc="MM")
                mm_of[kc] = cnt["MM"]

            def emit_exp(kc):
                psS = [ps3, ps4][kc % 2]
                pbuf = kc % 2
                if qt == 0 and kc == 0:
                    wt("scalar", "VV", biasEO_mark[0])
                wt("scalar", "MM", max(mm_of[kc], av_mm_for_buf[pbuf]))
                op("scalar", lambda e, pb=PB[pbuf], pS=psS: e.activation(
                    pb[:, 0:1024], pS[:, 0:1024], AF.Exp, bias=biasEO[:, 0:1],
                    scale=0.125), inc="SS")
                exp_of[kc] = cnt["SS"]
                exp_hist[0] = cnt["SS"]
                if kc >= 4 * qt:
                    off = (kc - 4 * qt) * 128
                    wt("vector", "SS", exp_of[kc])
                    jobs = []
                    for ho in (0, 512):
                        if off > 0:
                            jobs.append(("z", ho))
                        jobs.append(("m", ho))
                    for i, (kind, ho) in enumerate(jobs):
                        lastj = (i == len(jobs) - 1)
                        if kind == "z":
                            op("vector", lambda e, pb=PB[pbuf], off=off, ho=ho:
                                e.memset(pb[:, ho:ho + off], 0.0),
                                inc="VV" if lastj else None)
                        else:
                            op("vector", lambda e, pb=PB[pbuf], off=off, ho=ho:
                                e.tensor_tensor(
                                    pb[:, ho + off:ho + off + 128],
                                    pb[:, ho + off:ho + off + 128],
                                    tri[:, :], OP.mult),
                                inc="VV" if lastj else None)
                    pbr_of[kc] = ("VV", cnt["VV"])
                else:
                    pbr_of[kc] = ("SS", exp_of[kc])

            def emit_av(kc):
                pbuf = kc % 2
                first = (kc == 0)
                last = (kc == kmax - 1)
                if first:
                    wt("tensor", "VV", mul_mark[qt % 2])
                    if qt == 0:
                        wt("tensor", "VV", biasEO_mark[0])
                    elif qt == 1:
                        wt("tensor", "VV", max(v_mark[0], v_mark[1]))
                wt("tensor", pbr_of[kc][0], pbr_of[kc][1])
                op("tensor", lambda e, pb=PB[pbuf], kc=kc, buf=buf, f=first,
                   l=last, pE=pE: e.matmul(
                    pE[0:65, 0:512], Vt[:, kc, 0:65],
                    pb[:, 0:512], start=f, stop=l))
                op("tensor", lambda e, pb=PB[pbuf], kc=kc, buf=buf, f=first,
                   l=last, pO=pO: e.matmul(
                    pO[64:128, 0:512], Vt[:, kc, 66:130],
                    pb[:, 512:1024], start=f, stop=l))
                op("tensor", lambda e, pb=PB[pbuf], f=first, l=last, pO=pO:
                    e.matmul(
                    pO[0:1, 0:512], onescol128b[:, 0:1],
                    pb[:, 512:1024], start=f, stop=l), inc="MM")
                av_mm_for_buf[pbuf] = cnt["MM"]

            emit_phase2(0)
            emit_exp(0)
            if prev_normB[0]:
                prev_normB[0]()
                prev_normB[0] = None
            if kmax > 1:
                emit_phase2(1)
                emit_exp(1)
            for kc in range(kmax):
                if kc + 2 < kmax:
                    emit_phase2(kc + 2)
                emit_av(kc)
                if kc % 2 == 1 and kc < 4 * qt - 2:
                    qpump(1)
                if kc + 2 < kmax:
                    emit_exp(kc + 2)
            rr = recrow
            rx = recexp
            wt("scalar", "MM", cnt["MM"])
            if rrfree[0]:
                wt("scalar", "MM", rrfree[0])
            op("scalar", lambda e, pE=pE, rr=rr: e.activation(
                rr[64:65, 0:512], pE[64:65, 0:512], AF.Ln))
            op("scalar", lambda e, pO=pO, rr=rr: e.activation(
                rr[0:1, 0:512], pO[0:1, 0:512], AF.Ln), inc="SS")
            ln_mark = cnt["SS"]

            def norm_B(qt=qt, pE=pE, pO=pO, rr=rr, rx=rx, p=p, q0=q0,
                       lm=ln_mark, mprev=max(mul_mark[0], mul_mark[1])):
                wt("tensor", "SS", lm)
                op("tensor", lambda e, pO=pO, rr=rr: e.matmul(
                    pO[0:64, 0:512], onespart[64:65, 0:64],
                    rr[64:65, 0:512], start=True, stop=True))
                op("tensor", lambda e, pE=pE, rr=rr: e.matmul(
                    pE[64:128, 0:512], onespart[0:1, 0:64],
                    rr[0:1, 0:512], start=True, stop=True), inc="MM")
                rrfree[0] = cnt["MM"]
                wt("scalar", "MM", cnt["MM"])
                if mprev:
                    wt("scalar", "VV", mprev)
                op("scalar", lambda e, pO=pO, rx=rx: e.activation(
                    rx[0:64, :], pO[0:64, 0:512], AF.Exp, scale=-1.0))
                op("scalar", lambda e, pE=pE, rx=rx: e.activation(
                    rx[64:128, :], pE[64:128, 0:512], AF.Exp, scale=-1.0),
                    inc="SS")
                wt("vector", "SS", cnt["SS"])
                op("vector", lambda e, p=p, q0=q0, pE=pE, rx=rx: e.tensor_mul(
                    aout[0:64, p, q0:q0 + 512], pE[0:64, 0:512], rx[0:64, :]))
                op("vector", lambda e, p=p, q0=q0, pO=pO, rx=rx: e.tensor_mul(
                    aout[64:128, p, q0:q0 + 512], pO[64:128, 0:512],
                    rx[64:128, :]), inc="VV")
                mul_mark[qt % 2] = cnt["VV"]

            if qt < NTT - 1:
                prev_normB[0] = norm_B
            else:
                norm_pend[0] = norm_B
            qpump(2)
            qpumpB(1)

    if norm_pend[0]:
        norm_pend[0]()
        norm_pend[0] = None
    qpump(1000, fin=True)
    barrier()
    stA1.close()
    wps = sb("wps", (128, NCH, D), BF16, stA)
    wpdma = {}
    for h_ in range(4):
        wpdma[h_] = (f"W{h_}", dma(
            "gpsimd", wps[:, :, h_ * 256:(h_ + 1) * 256],
            A(d["wproj"], h_ * 256, [[D, 128], [128 * D, NCH], [1, 256]]),
            f"W{h_}"))
    proj_mark = {0: 0, 1: 0}
    for oc in range(NCH):
        wt("tensor", *wpdma[oc // 2])
        for tt in range(NTT):
            bk = (oc * NTT + tt) % 2
            psm = [ps1, ps2][bk]
            if proj_mark[bk]:
                wt("tensor", "VV", proj_mark[bk])
            for c in range(NCH):
                op("tensor", lambda e, c=c, oc=oc, tt=tt, p2=psm: e.matmul(
                    p2[:, 0:512], wps[:, c, oc * 128:(oc + 1) * 128],
                    aout[:, c, tt * 512:(tt + 1) * 512],
                    start=(c == 0), stop=(c == NCH - 1)),
                    inc="MM" if c == NCH - 1 else None)
            wt("vector", "MM", cnt["MM"])
            op("vector", lambda e, oc=oc, tt=tt, p2=psm: e.scalar_tensor_tensor(
                xs[:, oc, tt * 512:(tt + 1) * 512], p2[:, 0:512],
                bproj_c[:, oc:oc + 1], xs[:, oc, tt * 512:(tt + 1) * 512],
                OP.add, OP.add), inc="VV")
            proj_mark[bk] = cnt["VV"]
            qpumpB(2)
    if dbg:
        wt("sync", "VV", cnt["VV"])
        dma("sync", dbgt["dbg_aout"][:, :, :], aout[:, :, :], "DI")
        dma("sync", dbgt["dbg_x1"][:, :, :], xs[:, :, :], "DI")
        wt("tensor", "DI", cnt["DI"])
    qpumpB(1000)
    barrier()
    stA.close()

    v_ln2 = layernorm()

    stQF = contextlib.ExitStack()
    B2fc = sb("B2fc", (16, FF), BF16, stQF)
    B2pj = sb("B2pj", (16, D), BF16, stQF)
    tfc = sb("tfc", (16, T), BF16, stQF)
    stQB = contextlib.ExitStack()
    Bstg = sb("Bstg", (16, FF), F32, stQB)
    Bstg2 = sb("Bstg2", (16, D), F32, stQB)
    bpj2_row = sb("bpj2_row", (1, D), F32, stQB)
    t_cst = dma("sync", bpj2_row[:, :], A(d["bpj2"], 0, [[D, 1], [1, D]]), "DI")
    wt("sync", "VV", cnt["VV"])
    t_Bfc = dma("sync", Bstg[:, :], d["BfcT"][:, :], "DI")
    t_Bpj = dma("sync", Bstg2[:, :], d["BpjT"][:, :], "DI")
    for w, (srcB, dstB, n, tB) in enumerate([(Bstg, B2fc, FF, t_Bfc),
                                             (Bstg2, B2pj, D, t_Bpj)]):
        op("vector", lambda e, w=w: e.tensor_scalar(
            qrow[0:1, 6:7], qrow[0:1, 2 * w:2 * w + 1], 2.0, None, OP.mult),
            inc="VV")
        wt("tensor", "VV", cnt["VV"])
        op("tensor", lambda e: e.matmul(ps1[0:16, 0:1], onespart[0:1, 0:16],
                                        qrow[0:1, 6:7], start=True, stop=True),
           inc="MM")
        wt("vector", "MM", cnt["MM"])
        op("vector", lambda e: e.tensor_copy(qsc[0:16, 6:7], ps1[0:16, 0:1]),
           inc="VV")
        wt("vector", "DI", cnt["DI"])
        wt("vector", "VV", cnt["VV"])
        op("vector", lambda e, srcB=srcB, dstB=dstB, n=n: e.tensor_scalar(
            dstB[:, 0:n], srcB[:, 0:n], qsc[0:16, 6:7], None, OP.mult),
            inc="VV")
    wt("vector", "DI", cnt["DI"])
    op("vector", lambda e: e.tensor_scalar(
        bpj2_rowb[:, :], bpj2_row[:, :], qrow[0:1, 2:3], None, OP.mult), inc="VV")
    v_prep = cnt["VV"]

    wt("tensor", "VV", v_ln2)
    for tt in range(NTT):
        wt("tensor", "VV", cnt["VV"])
        for c in range(NCH):
            op("tensor", lambda e, c=c, tt=tt: e.matmul(
                ps5[0:16, 0:512], afc_s[:, c, :],
                xln[:, c, tt * 512:(tt + 1) * 512],
                start=(c == 0), stop=(c == NCH - 1)),
                inc="MM" if c == NCH - 1 else None)
        wt("vector", "MM", cnt["MM"])
        op("vector", lambda e, tt=tt: e.tensor_copy(
            tfc[:, tt * 512:(tt + 1) * 512], ps5[0:16, 0:512]), inc="VV")
    wt("tensor", "VV", cnt["VV"])
    if dbg:
        wt("sync", "VV", cnt["VV"])
        dma("sync", dbgt["dbg_qsc"][:, :], qsc[:, :], "DI")
        dma("sync", dbgt["dbg_qrow"][:, :], qrow[:, :], "DI")
        dma("sync", dbgt["dbg_tfc"][:, :], tfc[:, :], "DI")
        wt("tensor", "DI", cnt["DI"])
    barrier()
    stQB.close()

    stF = contextlib.ExitStack()
    ht = sb("ht", (128, NFC, 1024), BF16, stF)
    tpj = sb("tpj", (16, 1024), BF16, stF)
    wfq = [sb(f"wfq{i}", (128, NCH, 128), BF16, stF) for i in range(2)]
    wpq = [sb(f"wpq{i}", (128, NFC, 128), BF16, stF) for i in range(2)]

    wt("scalar", "VV", v_prep)
    fcq_sem = ["DA", "DB"]
    pjq_sem = ["DC", "DD"]
    gelu_mark = {0: 0, 1: 0}
    stt_mark = {0: 0, 1: 0, 2: 0, 3: 0}
    for tt2 in range(2):
        tb0 = tt2 * 1024
        wt("scalar", "MM", cnt["MM"])
        fcq_t = {}
        wt("sync", "DI", cnt["DI"])
        wt("sync", "MM", cnt["MM"])
        for j in range(2):
            fcq_t[j] = dma("sync", wfq[j][:, :, :],
                           A(wfcq, j * 128, [[FF, 128], [128 * FF, NCH], [1, 128]]),
                           fcq_sem[j])
        for f in range(NFC):
            wt("tensor", fcq_sem[f % 2], fcq_t[f])
            for half in range(2):
                psm = [ps1, ps2][half]
                h0 = tb0 + half * 512
                wt("tensor", "SS", gelu_mark[half])
                for c in range(NCH):
                    op("tensor", lambda e, c=c, f=f, h0=h0, p2=psm: e.matmul(
                        p2[:, 0:512], wfq[f % 2][:, c, :], xln[:, c, h0:h0 + 512],
                        start=(c == 0), stop=False))
                op("tensor", lambda e, f=f, h0=h0, p2=psm: e.matmul(
                    p2[:, 0:512], B2fc[:, f * 128:(f + 1) * 128],
                    tfc[:, h0:h0 + 512], start=False, stop=True), inc="MM")
                wt("scalar", "MM", cnt["MM"])
                op("scalar", lambda e, f=f, half=half, p2=psm: e.activation(
                    ht[:, f, half * 512:(half + 1) * 512], p2[:, 0:512], AF.Gelu,
                    bias=bfc_c[:, f:f + 1], scale=qsc[:, 1:2]), inc="SS")
                gelu_mark[half] = cnt["SS"]
            if f + 2 < NFC:
                wt("sync", "MM", cnt["MM"])
                fcq_t[f + 2] = dma("sync", wfq[f % 2][:, :, :],
                                   A(wfcq, (f + 2) * 128,
                                     [[FF, 128], [128 * FF, NCH], [1, 128]]),
                                   fcq_sem[f % 2])
        wt("tensor", "SS", cnt["SS"])
        wt("tensor", "DG", cnt["DG"])
        for half in range(2):
            h0 = half * 512
            wt("tensor", "VV", cnt["VV"])
            for f in range(NFC):
                op("tensor", lambda e, f=f, h0=h0: e.matmul(
                    ps5[0:16, 0:512], apj_s[:, f, :], ht[:, f, h0:h0 + 512],
                    start=(f == 0), stop=(f == NFC - 1)),
                    inc="MM" if f == NFC - 1 else None)
            wt("vector", "MM", cnt["MM"])
            op("vector", lambda e, h0=h0: e.tensor_copy(
                tpj[:, h0:h0 + 512], ps5[0:16, 0:512]), inc="VV")
        wt("tensor", "VV", cnt["VV"])
        pjq_t = {}
        wt("sync", "DI", cnt["DI"])
        wt("sync", "MM", cnt["MM"])
        for j in range(2):
            pjq_t[j] = dma("sync", wpq[j][:, :, :],
                           A(wpqq, j * 128, [[D, 128], [128 * D, NFC], [1, 128]]),
                           pjq_sem[j])
        for oc in range(NCH):
            wt("tensor", pjq_sem[oc % 2], pjq_t[oc])
            for half in range(2):
                slot = (oc * 2 + half) % 4
                psm = [ps3, ps4][slot // 2]
                sl0 = (slot % 2) * 512
                h0 = half * 512
                wt("tensor", "VV", stt_mark[slot])
                for f in range(NFC):
                    op("tensor", lambda e, f=f, oc=oc, h0=h0, p2=psm, sl0=sl0:
                        e.matmul(p2[:, sl0:sl0 + 512], wpq[oc % 2][:, f, :],
                                 ht[:, f, h0:h0 + 512], start=(f == 0),
                                 stop=False))
                op("tensor", lambda e, oc=oc, h0=h0, p2=psm, sl0=sl0: e.matmul(
                    p2[:, sl0:sl0 + 512], B2pj[:, oc * 128:(oc + 1) * 128],
                    tpj[:, h0:h0 + 512], start=False, stop=False))
                op("tensor", lambda e, oc=oc, p2=psm, sl0=sl0: e.matmul(
                    p2[:, sl0:sl0 + 512], bpj2_rowb[0:1, oc * 128:(oc + 1) * 128],
                    onesrow512b[0:1, 0:512], start=False, stop=True), inc="MM")
                wt("vector", "MM", cnt["MM"])
                op("vector", lambda e, oc=oc, tt2=tt2, h0=h0, p2=psm, sl0=sl0:
                    e.scalar_tensor_tensor(
                        xs[:, oc, tt2 * 1024 + h0:tt2 * 1024 + h0 + 512],
                        p2[:, sl0:sl0 + 512], qsc[:, 3:4],
                        xs[:, oc, tt2 * 1024 + h0:tt2 * 1024 + h0 + 512],
                        OP.mult, OP.add), inc="VV")
                stt_mark[slot] = cnt["VV"]
            if oc + 2 < NCH:
                wt("sync", "MM", cnt["MM"])
                pjq_t[oc + 2] = dma("sync", wpq[oc % 2][:, :, :],
                                    A(wpqq, (oc + 2) * 128,
                                      [[D, 128], [128 * D, NFC], [1, 128]]),
                                    pjq_sem[oc % 2])

    wt("sync", "VV", cnt["VV"])
    t_out = dma("sync", A(yT, 0, [[T, 128], [128 * T, NCH], [1, T]]),
                xs[:, :, :], "DI")
    wt("sync", "DI", cnt["DI"])
    wt("gpsimd", "DI", cnt["DI"])
    wt("tensor", "DI", cnt["DI"])
    stF.close()
    stQF.close()

    with nc.Block() as blk:
        @blk.tensor
        def _(e):
            for fn in prog["tensor"]:
                fn(e)

        @blk.vector
        def _(e):
            for fn in prog["vector"]:
                fn(e)

        @blk.scalar
        def _(e):
            for fn in prog["scalar"]:
                fn(e)

        @blk.gpsimd
        def _(e):
            for fn in prog["gpsimd"]:
                fn(e)

        @blk.sync
        def _(e):
            for fn in prog["sync"]:
                fn(e)

    st.close()
    return nc


def _oneblk33():
    m = np.zeros((128, 33), dtype=np.float32)
    m[0:64, 0] = 1.0
    m[64:128, 32] = 1.0
    return m.astype(BF)


def _consts():
    return dict(
        tri=np.triu(np.ones((128, 128), dtype=np.float32)).astype(BF),
        iden=np.eye(128, dtype=np.float32),
        onespart=np.ones((128, 128), dtype=np.float32),
        onescolb=np.ones((1, 128), dtype=BF),
        onescol128b=np.ones((128, 1), dtype=BF),
        oneblk=_oneblk33(),
        o1024col=np.full((128, 1), 1.0 / 1024.0, dtype=np.float32),
        o1024colb=np.full((128, 1), 1.0 / 1024.0, dtype=BF),
        onesrow512b=np.ones((1, 512), dtype=BF),
    )


def kernel(L0_attn_x, L0_attn_w_qkv, L0_attn_b_qkv, L0_attn_w_proj, L0_attn_b_proj,
           L0_ffn_W_fc, L0_ffn_b_fc, L0_ffn_A_fc, L0_ffn_B_fc, L0_ffn_W_proj,
           L0_ffn_b_proj, L0_ffn_A_proj, L0_ffn_B_proj):
    x = np.asarray(L0_attn_x, dtype=np.float32)
    shared = dict(
        wqkv=np.ascontiguousarray(np.asarray(L0_attn_w_qkv, np.float32)),
        bqkv=np.ascontiguousarray(np.asarray(L0_attn_b_qkv, np.float32)),
        wproj=np.ascontiguousarray(np.asarray(L0_attn_w_proj, np.float32)),
        bproj=np.ascontiguousarray(np.asarray(L0_attn_b_proj, np.float32)),
        wfcT=np.ascontiguousarray(np.asarray(L0_ffn_W_fc, np.float32).T),
        bfc=np.ascontiguousarray(np.asarray(L0_ffn_b_fc, np.float32)),
        afcT=np.ascontiguousarray(np.asarray(L0_ffn_A_fc, np.float32).T),
        BfcT=np.ascontiguousarray(np.asarray(L0_ffn_B_fc, np.float32).T),
        wpjT=np.ascontiguousarray(np.asarray(L0_ffn_W_proj, np.float32).T),
        bpj2=np.ascontiguousarray(np.asarray(L0_ffn_b_proj, np.float32)),
        apjT=np.ascontiguousarray(np.asarray(L0_ffn_A_proj, np.float32).T),
        BpjT=np.ascontiguousarray(np.asarray(L0_ffn_B_proj, np.float32).T),
        **_consts(),
    )
    nc = build_nc()
    in_maps = []
    for c in range(B):
        m = dict(shared)
        m["xT"] = np.ascontiguousarray(x[c].T)
        in_maps.append(m)
    res = run_bass_kernel_spmd(nc, in_maps, core_ids=list(range(B)))
    out = np.stack([np.ascontiguousarray(np.asarray(r["yT"]).T)
                    for r in res.results], axis=0)
    return out.astype(np.float32)
